# version 11
# baseline (speedup 1.0000x reference)
import math
import numpy as np
import ml_dtypes
import concourse.bass as bass
import concourse.mybir as mybir
from concourse.bass_utils import run_bass_kernel_spmd
from concourse.dve_op import DveOp
from concourse.dve_spec import (
    Spec, Src0, Src1, C0, C1, C2, C3, Zero, One, maxx, lower,
)
from concourse.dve_table_gen import dve_ver_for
import concourse.dve_ops as dve_ops_mod

F32 = mybir.dt.float32
BF16 = mybir.dt.bfloat16
AF = mybir.ActivationFunctionType
ALU = mybir.AluOpType
AX = mybir.AxisListType
BF = ml_dtypes.bfloat16


def _make_tables():
    import jax
    jax.config.update("jax_enable_x64", True)
    import jax.numpy as jnp
    with jax.default_device(jax.devices("cpu")[0]):
        inv_freq = 1.0 / 10000.0 ** (jnp.arange(0, 128, 2, dtype=jnp.float32) / 128)
        freqs = jnp.arange(2048, dtype=jnp.float32)[:, None] * inv_freq[None, :]
        emb = jnp.concatenate([freqs, freqs], axis=-1)
        cos = np.asarray(jnp.cos(emb), np.float32)
        sin = np.asarray(jnp.sin(emb), np.float32)
        u = np.asarray(jax.random.uniform(
            jax.random.key(42, impl="threefry2x32"), (4, 16), jnp.float64))
    return cos, sin, u


class Ctr:
    def __init__(self, sem, name="", step=1):
        self.sem = sem
        self.n = 0
        self.name = name
        self.step = step


class Buf:
    __slots__ = ("name", "last_write", "readers")

    def __init__(self, name):
        self.name = name
        self.last_write = None
        self.readers = []


class Prog:
    def __init__(self, settle_fns=None, settle_bufs=None):
        self.steps = []
        self.floor = {}
        self.settle_fns = settle_fns or {}
        self.settle_bufs = settle_bufs or {}

    def barrier(self, ctrs):
        for c in ctrs:
            if c.n > 0:
                self.floor[c] = c.n

    def step(self, engine, emit, reads=(), writes=(), extra_waits=(), inc=None,
             ninc=1, settle=False, tag=""):
        if settle and engine in self.settle_bufs:
            jin, jout = self.settle_bufs[engine]
            reads = list(reads) + [jin]
            writes = list(writes) + [jout]
        waits = {}

        def add(m):
            if m is None:
                return
            c, v = m
            if v > 0:
                waits[c] = max(waits.get(c, 0), v)
        for b in reads:
            add(b.last_write)
        for b in writes:
            add(b.last_write)
            for m in b.readers:
                add(m)
        for m in extra_waits:
            add(m)
        for c, v in self.floor.items():
            if v > 0:
                waits[c] = max(waits.get(c, 0), v)
        if inc is not None and inc.step == 16:
            add((inc, inc.n))
        if inc is not None:
            inc.n += inc.step * ninc
            marker = (inc, inc.n)
        else:
            marker = None
        self.steps.append((engine, emit, tuple(waits.items()), inc, ninc,
                           settle, tag))
        if marker is not None:
            for b in writes:
                b.last_write = marker
                b.readers = []
            for b in reads:
                b.readers.append(marker)
        else:
            assert not writes, f"step {tag}: writes declared but no inc"
            assert not reads, f"step {tag}: reads declared but no inc (WAR untrackable)"
        return marker

    def emit(self, engine_name, eng):
        n = 0
        seen = {}
        for engine, emit, waits, inc, ninc, settle, tag in self.steps:
            if engine != engine_name:
                continue
            for c, v in waits:
                if v > 0 and seen.get(c, -1) < v:
                    eng.wait_ge(c.sem, v)
                    seen[c] = v
            res = emit(eng)
            if settle and inc is not None:
                assert ninc == 1, f"step {tag}: settle with ninc>1"
                res = self.settle_fns[engine_name](eng)
            if inc is not None:
                insts = list(res) if isinstance(res, (list, tuple)) else [res]
                assert len(insts) == ninc and all(i is not None for i in insts), (
                    f"step {tag}: ninc={ninc} but emit returned {len(insts)}")
                for i in insts:
                    i.then_inc(inc.sem, inc.step)
            n += 1
        return n


_H2 = One + One
_H4 = _H2 * _H2
_H8 = _H4 * _H2
_H16 = _H4 * _H4


def _ref_qexp_a(in0, in1, c0, c1, c2):
    a = in0 * c0 + c1
    r8 = (a >= 8.0).astype(a.dtype)
    rem = a - r8 * 8.0
    r4 = (rem >= 4.0).astype(a.dtype)
    return rem - r4 * 4.0


def _ref_qexp_b(in0, in1, c0, c1, c2):
    rem = in0
    r2 = (rem >= 2.0).astype(rem.dtype)
    rem = rem - r2 * 2.0
    r1 = (rem >= 1.0).astype(rem.dtype)
    rem4 = rem - r1
    return (in1 * c0 + c1) - rem4


def _ref_qlad_a(in0, in1, c0, c1, c2):
    w = in0 * in1
    aw = np.maximum(w, -w)
    t = aw + c0
    r16 = (t >= 16.0).astype(t.dtype)
    return t - r16 * 16.0


def _ref_qlad_b(in0, in1, c0, c1, c2):
    rem = in0
    r8 = (rem >= 8.0).astype(rem.dtype)
    rem = rem - r8 * 8.0
    r4 = (rem >= 4.0).astype(rem.dtype)
    return rem - r4 * 4.0


def _ref_qlad_c(in0, in1, c0, c1, c2):
    rem = in0
    r2 = (rem >= 2.0).astype(rem.dtype)
    rem = rem - r2 * 2.0
    r1 = (rem >= 1.0).astype(rem.dtype)
    rem = rem - r1
    return (c0 - rem) * in1


def _ref_qdeq(in0, in1, c0, c1, c2):
    ds = in0 * c0
    sg = (in1 >= 0.0).astype(in0.dtype)
    sg3 = (sg + sg) - 1.0
    return in1 + sg3 * ds


def _spec_qexp_a():
    a = Src0 * C0 + C1
    r8 = a >= C2
    rem = a - r8 * C2
    r4 = rem >= C3
    return Spec(body=rem - r4 * C3)


def _spec_qexp_b():
    r2 = Src0 >= C2
    rem = Src0 - r2 * C2
    r1 = rem >= One
    rem4 = rem - r1
    return Spec(body=(Src1 * C0 + C1) - rem4)


def _spec_qlad_a():
    w = Src0 * Src1
    aw = maxx(w, Zero - w)
    t = aw + C0
    r16 = t >= C2
    return Spec(body=t - r16 * C2)


def _spec_qlad_b():
    r8 = Src0 >= C0
    rem = Src0 - r8 * C0
    r4 = rem >= C1
    return Spec(body=rem - r4 * C1)


def _spec_qlad_c():
    r2 = Src0 >= C1
    rem = Src0 - r2 * C1
    r1 = rem >= One
    rem2 = rem - r1
    return Spec(body=(C0 - rem2) * Src1)


def _spec_qdeq():
    ds = Src0 * C0
    sg = Src1 >= Zero
    sg3 = (sg + sg) - One
    return Spec(body=Src1 + sg3 * ds)


_DEFS = [
    ("ANT_QEXP_A", _spec_qexp_a, _ref_qexp_a),
    ("ANT_QEXP_B", _spec_qexp_b, _ref_qexp_b),
    ("ANT_QLAD_A", _spec_qlad_a, _ref_qlad_a),
    ("ANT_QLAD_B", _spec_qlad_b, _ref_qlad_b),
    ("ANT_QLAD_C", _spec_qlad_c, _ref_qlad_c),
    ("ANT_QDEQ", _spec_qdeq, _ref_qdeq),
]


def register_quant_ops():
    out = {}
    ver = dve_ver_for("TRN2")
    if [o.name for o in dve_ops_mod.OPS[:len(_DEFS)]] == [d[0] for d in _DEFS]:
        return {o.name: o for o in dve_ops_mod.OPS[:len(_DEFS)]}
    new_ops = []
    for name, mk_spec, ref in _DEFS:
        spec = mk_spec()
        compiled = lower(spec, ver=ver, perf_en=False, name=name)
        sha = compiled.sha(ver)
        op = DveOp(name, spec, subdim=False, uops_sha={ver: sha}, reference=ref)
        new_ops.append(op)
        dve_ops_mod.CUSTOM_DVE_SPECS[name] = spec
        dve_ops_mod.CUSTOM_DVE_REFS[name] = ref
        out[name] = op
    dve_ops_mod.OPS[:] = new_ops
    dve_ops_mod._SUB_OPCODE_FOR_NAME.clear()
    dve_ops_mod._SUB_OPCODE_FOR_NAME.update(
        {op.name: dve_ops_mod._CUSTOM_DVE_ROW_BASE + i for i, op in enumerate(new_ops)})
    return out




P = 128
NQ = 1024
NDC = 16
H = 16
NKC = 8
NTT = 2


def build(upto="all", dbg=()):
    ops = register_quant_ops()
    nc = bass.Bass("TRN2", target_bir_lowering=False)
    dbg = set(dbg)
    dbg_out = {}

    d_xT = nc.dram_tensor("xT", [2048, NQ], F32, kind="ExternalInput")
    d_xkT = nc.dram_tensor("xkT", [2048, NQ], F32, kind="ExternalInput")
    dW = {nm: nc.dram_tensor(nm, [2048, 2048], F32, kind="ExternalInput")
          for nm in ["WqT", "WkT", "WvT", "WoT", "Wf"]}
    d_cosq = nc.dram_tensor("cosq", [P, NQ], F32, kind="ExternalInput")
    d_sinq = nc.dram_tensor("sinq", [P, NQ], F32, kind="ExternalInput")
    d_cosk = nc.dram_tensor("cosk", [P, NQ], F32, kind="ExternalInput")
    d_sink = nc.dram_tensor("sink", [P, NQ], F32, kind="ExternalInput")
    d_mask = nc.dram_tensor("mask", [P, 2 * NKC], BF16, kind="ExternalInput")
    d_urow = nc.dram_tensor("urow", [1, H], F32, kind="ExternalInput")
    d_eye = nc.dram_tensor("eyef", [P, P], F32, kind="ExternalInput")
    d_out = nc.dram_tensor("outT", [2048, NQ], F32, kind="ExternalOutput")
    d_wscr = nc.dram_tensor("wscr", [2048, 2048], BF16, kind="Internal")

    from contextlib import ExitStack
    es = ExitStack()
    SV = Ctr(es.enter_context(nc.semaphore("sv")), "vector")
    SS = Ctr(es.enter_context(nc.semaphore("ss")), "scalar")
    SP = Ctr(es.enter_context(nc.semaphore("sp")), "tensor")
    SD = Ctr(es.enter_context(nc.semaphore("sd")), "sync", step=16)
    SG = Ctr(es.enter_context(nc.semaphore("sg")), "gpsimd", step=16)
    SL = Ctr(es.enter_context(nc.semaphore("sl")), "gpsimd")
    t_jk1 = es.enter_context(nc.sbuf_tensor("jk1", [P, 128], F32, side="left"))
    t_jv2 = es.enter_context(nc.sbuf_tensor("jv2", [P, 128], F32, side="left"))
    t_js2 = es.enter_context(nc.sbuf_tensor("js2", [P, 128], F32, side="left"))
    t_jg2 = es.enter_context(nc.sbuf_tensor("jg2", [P, 128], F32, side="left"))
    b_jk1 = Buf("jk1")
    b_jv2, b_js2, b_jg2 = Buf("jv2"), Buf("js2"), Buf("jg2")
    _settle = {
        "vector": lambda e: e.tensor_copy(t_jv2[:], t_jk1[:]),
        "scalar": lambda e: e.activation(t_js2[:], t_jk1[:], AF.Square),
        "gpsimd": lambda e: e.tensor_copy(t_jg2[0:64, :], t_jk1[0:64, :]),
    }
    pr = Prog(settle_fns=_settle,
              settle_bufs={"vector": (b_jk1, b_jv2), "scalar": (b_jk1, b_js2),
                           "gpsimd": (b_jk1, b_jg2)})

    def sbuf(name, shape, dt=F32, stack=es, side="right"):
        t = stack.enter_context(nc.sbuf_tensor(name, shape, dt, side=side))
        return t, Buf(name)

    def psum(name, shape, dt=F32, stack=es):
        t = stack.enter_context(nc.psum_tensor(name, shape, dt))
        return t, Buf(name)

    def tap(name, t, b, stack=es):
        if name not in dbg:
            return
        dh = nc.dram_tensor("dbg_" + name, list(t.shape), t.dtype, kind="ExternalOutput")
        dbg_out[name] = dh
        pr.step("sync", lambda e: e.dma_start(dh[:, :], t[:]), reads=[b], inc=SD,
                tag="tap_" + name)

    cN = {}
    cB = {}
    for nm, v in [("c2", 2.0), ("c448", 448.0), ("c4", 4.0), ("b35", 1e-35),
                  ("ceps", 1e-5), ("c1e6", 1e-6), ("cmh", -0.5)]:
        cN[nm], cB[nm] = sbuf(nm, [P, 1], side="left")
    for nm in ["ln2c", "iln2", "ccol", "sln", "bm9", "bp9", "rsqd", "r448",
               "r818", "r819"]:
        cN[nm], cB[nm] = sbuf(nm, [P, 1], side="left")
    t_eye, b_eye = sbuf("eye", [P, P], F32, side="left")
    t_ones1, b_ones1 = sbuf("ones1", [P, 1], BF16, side="left")
    t_onesr, b_onesr = sbuf("onesr", [1, P], F32, side="left")
    t_mask, b_mask = sbuf("maskT", [P, 2 * NKC], BF16, side="left")
    t_urow, b_urow = sbuf("urowT", [1, H], F32, side="left")

    def s0():
        def e_ms(e):
            for nm, v in [("c2", 2.0), ("c448", 448.0), ("c4", 4.0), ("b35", 1e-35),
                          ("ceps", 1e-5), ("c1e6", 1e-6), ("cmh", -0.5),
                          ("r818", 818.0), ("r819", 819.0)]:
                inst = e.memset(cN[nm][:], v)
            inst = e.memset(t_ones1[:], 1.0)
            inst = e.memset(t_onesr[:], 1.0)
            return e.memset(t_jk1[:], 1.0)
        pr.step("vector", e_ms, writes=[cB[n] for n in
                ["c2", "c448", "c4", "b35", "ceps", "c1e6", "cmh", "r818", "r819"]]
                + [b_ones1, b_onesr, b_jk1], inc=SV, tag="memsets")
        pr.step("scalar", lambda e: e.activation(cN["ln2c"][:], cN["c2"][:], AF.Ln),
                reads=[cB["c2"]], writes=[cB["ln2c"]], inc=SS, tag="ln2")
        pr.step("vector", lambda e: e.reciprocal(cN["iln2"][:], cN["ln2c"][:]),
                reads=[cB["ln2c"]], writes=[cB["iln2"]], inc=SV)
        def e_derived(e):
            e.tensor_scalar(cN["ccol"][:], cN["iln2"][:], 0.5, None, ALU.mult)
            e.tensor_scalar(cN["sln"][:], cN["ln2c"][:], -1.0, None, ALU.mult)
            e.tensor_scalar(cN["bm9"][:], cN["ln2c"][:], -9.0, None, ALU.mult)
            return e.tensor_scalar(cN["bp9"][:], cN["ln2c"][:], 9.0, None, ALU.mult)
        pr.step("vector", e_derived, reads=[cB["iln2"], cB["ln2c"]],
                writes=[cB["ccol"], cB["sln"], cB["bm9"], cB["bp9"]], inc=SV)
        pr.step("scalar", lambda e: e.activation(cN["rsqd"][:], cN["ln2c"][:], AF.Exp, scale=-3.5),
                reads=[cB["ln2c"]], writes=[cB["rsqd"]], inc=SS, tag="rsqd")
        def e_recips(e):
            e.reciprocal(cN["r448"][:], cN["c448"][:])
            e.reciprocal(cN["r818"][:], cN["r818"][:])
            return e.reciprocal(cN["r819"][:], cN["r819"][:])
        pr.step("vector", e_recips, reads=[cB["c448"], cB["r818"], cB["r819"]],
                writes=[cB["r448"], cB["r818"], cB["r819"]], inc=SV)
        def e_lddma(e):
            return [e.dma_start(t_eye[:], d_eye[:, :]),
                    e.dma_start(t_mask[:], d_mask[:, :]),
                    e.dma_start(t_urow[:], d_urow[:, :])]
        pr.step("sync", e_lddma, writes=[b_eye, b_mask, b_urow], inc=SD, ninc=3,
                tag="s0dma")

    def s1_set(tag, d_in, t_out, b_out, st):
        t_xf, b_xf = sbuf(tag + "xf", [P, NDC * NQ], F32, st)
        t_xb, b_xb = sbuf(tag + "xb", [P, NDC * NQ], BF16, st)
        t_sq, b_sq = [None, None], [None, None]
        for i in range(2):
            t_sq[i], b_sq[i] = sbuf(tag + f"sq{i}", [P, 512], BF16, st)
        t_r0, b_r0 = sbuf(tag + "r0", [1, NQ], F32, st)
        t_r1, b_r1 = sbuf(tag + "r1", [1, NQ], F32, st)
        t_r2, b_r2 = sbuf(tag + "r2", [1, NQ], F32, st)
        t_r3, b_r3 = sbuf(tag + "r3", [1, NQ], F32, st)
        t_A, b_A = sbuf(tag + "A", [1, NQ], F32, st)
        t_Br, b_Br = sbuf(tag + "B", [1, NQ], F32, st)
        t_s1, b_s1 = sbuf(tag + "s1", [P, 8 * 512], F32, st)
        ps_s, pb_s = psum(tag + "pss", [1, 512], F32, st)
        ps_q, pb_q = psum(tag + "psq", [1, 512], F32, st)
        ps_A, pb_A = psum(tag + "psA", [P, 512], F32, st)
        ps_B, pb_B = psum(tag + "psB", [P, 512], F32, st)

        def e_ld(e):
            return [e.dma_start(t_xf[:, dc * NQ:(dc + 1) * NQ],
                                d_in[dc * P:(dc + 1) * P, :]) for dc in range(NDC)]
        pr.step("sync", e_ld, writes=[b_xf], inc=SD, ninc=NDC, tag=tag + "ld")

        def e_cast(e):
            for dc in range(NDC):
                inst = e.tensor_copy(t_xb[:, dc * NQ:(dc + 1) * NQ],
                                     t_xf[:, dc * NQ:(dc + 1) * NQ])
            return inst
        pr.step("vector", e_cast, reads=[b_xf], writes=[b_xb], inc=SV, tag=tag + "cast")

        for tt in range(NTT):
            for dc in range(NDC):
                par = dc % 2
                pr.step("scalar", lambda e, dc=dc, tt=tt, par=par: e.activation(
                    t_sq[par][:], t_xb[:, dc * NQ + tt * 512: dc * NQ + tt * 512 + 512],
                    AF.Square),
                    reads=[b_xb], writes=[b_sq[par]], inc=SS, tag=f"{tag}sq{tt}_{dc}")
                def e_mm(e, dc=dc, tt=tt, par=par):
                    e.matmul(ps_s[:, :], t_ones1[:],
                             t_xb[:, dc * NQ + tt * 512: dc * NQ + tt * 512 + 512],
                             start=(dc == 0), stop=(dc == NDC - 1))
                    return e.matmul(ps_q[:, :], t_ones1[:], t_sq[par][:],
                                    start=(dc == 0), stop=(dc == NDC - 1))
                pr.step("tensor", e_mm, reads=[b_xb, b_sq[par], b_ones1],
                        writes=[pb_s, pb_q] if dc == NDC - 1 else [], inc=SP,
                        tag=f"{tag}mm{tt}_{dc}")
            sl = slice(tt * 512, tt * 512 + 512)
            def e_rows(e, sl=sl):
                e.tensor_scalar(t_r0[0:1, sl], ps_s[:, :], 1.0 / 2048.0, None, ALU.mult)
                return e.tensor_scalar(t_r1[0:1, sl], ps_q[:, :], 1.0 / 2048.0, None, ALU.mult)
            pr.step("vector", e_rows, reads=[pb_s, pb_q], writes=[b_r0, b_r1], inc=SV,
                    tag=tag + "rows", settle=True)
            pr.step("scalar", lambda e, sl=sl: e.activation(t_r2[0:1, sl], t_r0[0:1, sl],
                                                            AF.Square),
                    reads=[b_r0], writes=[b_r2], inc=SS, tag=tag + "m2", settle=True)
            pr.step("vector", lambda e, sl=sl: e.tensor_sub(t_r2[0:1, sl], t_r1[0:1, sl],
                                                            t_r2[0:1, sl]),
                    reads=[b_r1, b_r2], writes=[b_r2], inc=SV, tag=tag + "var", settle=True)
            pr.step("scalar", lambda e, sl=sl: e.activation(t_r3[0:1, sl], t_r2[0:1, sl],
                                                            AF.Ln, bias=cN["ceps"][0:1, :]),
                    reads=[b_r2, cB["ceps"]], writes=[b_r3], inc=SS, tag=tag + "lnv", settle=True)
            pr.step("scalar", lambda e, sl=sl: e.activation(t_A[0:1, sl], t_r3[0:1, sl],
                                                            AF.Exp, scale=-0.5),
                    reads=[b_r3], writes=[b_A], inc=SS, tag=tag + "rstd", settle=True)
            pr.step("vector", lambda e, sl=sl: e.tensor_mul(t_Br[0:1, sl], t_r0[0:1, sl],
                                                            t_A[0:1, sl]),
                    reads=[b_r0, b_A], writes=[b_Br], inc=SV, tag=tag + "Br", settle=True)
            pr.step("tensor", lambda e, sl=sl: e.matmul(ps_A[:, :], t_onesr[:], t_A[0:1, sl],
                                                        start=True, stop=True),
                    reads=[b_A, b_onesr], writes=[pb_A], inc=SP, tag=tag + "bA")
            pr.step("tensor", lambda e, sl=sl: e.matmul(ps_B[:, :], t_onesr[:], t_Br[0:1, sl],
                                                        start=True, stop=True),
                    reads=[b_Br, b_onesr], writes=[pb_B], inc=SP, tag=tag + "bB")
            for half in range(2):
                dcs = range(half * 8, half * 8 + 8)
                def e_mul(e, dcs=dcs, tt=tt):
                    for j, dc in enumerate(dcs):
                        inst = e.tensor_tensor(t_s1[:, j * 512:(j + 1) * 512],
                                               t_xf[:, dc * NQ + tt * 512: dc * NQ + tt * 512 + 512],
                                               ps_A[:, :], ALU.mult)
                    return inst
                pr.step("vector", e_mul, reads=[b_xf, pb_A], writes=[b_s1], inc=SV,
                        tag=f"{tag}ap1_{tt}_{half}")
                def e_sub(e, dcs=dcs, tt=tt):
                    for j, dc in enumerate(dcs):
                        inst = e.tensor_tensor(
                            t_out[:, dc * NQ + tt * 512: dc * NQ + tt * 512 + 512],
                            t_s1[:, j * 512:(j + 1) * 512], ps_B[:, :], ALU.subtract)
                    return inst
                pr.step("vector", e_sub, reads=[b_s1, pb_B], writes=[b_out], inc=SV,
                        tag=f"{tag}ap2_{tt}_{half}")

    def proj_weight_half(st, dname, half, t_wbf, b_wbf, t_wst, b_wst, tagp):
        for dc in range(NDC):
            par = dc % 2
            pr.step("sync", lambda e, dc=dc, half=half, par=par: e.dma_start(
                t_wst[par][:], dW[dname][dc * P:(dc + 1) * P,
                                         half * NQ:(half + 1) * NQ]),
                writes=[b_wst[par]], inc=SD, tag=f"{tagp}ld{half}_{dc}")
            pr.step("vector", lambda e, dc=dc, par=par: e.tensor_copy(
                t_wbf[:, dc * NQ:(dc + 1) * NQ], t_wst[par][:]),
                reads=[b_wst[par]], writes=[b_wbf], inc=SV, tag=f"{tagp}cast{half}_{dc}")

    def s2_qk(st, dname, tagp, d_cos, d_sin, ln_t, ln_b, out_t, out_b):
        t_wst, b_wst = [None] * 2, [None] * 2
        for i in range(2):
            t_wst[i], b_wst[i] = sbuf(f"{tagp}wst{i}", [P, NQ], F32, st)
        t_wbf, b_wbf = sbuf(tagp + "wbf", [P, NDC * NQ], BF16, st)
        t_qraw, b_qraw = sbuf(tagp + "qraw", [P, 512], F32, st)
        t_qswp, b_qswp = sbuf(tagp + "qswp", [P, 512], F32, st)
        t_t1, b_t1 = sbuf(tagp + "t1", [P, 512], F32, st)
        t_t2, b_t2 = sbuf(tagp + "t2", [P, 512], F32, st)
        t_cq, b_cq = sbuf(tagp + "cosT", [P, NQ], F32, st)
        t_sn, b_sn = sbuf(tagp + "sinT", [P, NQ], F32, st)
        ps_p, pb_p = [None] * 2, [None] * 2
        for i in range(2):
            ps_p[i], pb_p[i] = psum(f"{tagp}ps{i}", [P, 512], F32, st)

        def e_tab(e):
            return [e.dma_start(t_cq[:], d_cos[:, :]),
                    e.dma_start(t_sn[:], d_sin[:, :])]
        pr.step("sync", e_tab, writes=[b_cq, b_sn], inc=SD, ninc=2, tag=tagp + "tab")

        for half in range(2):
            proj_weight_half(st, dname, half, t_wbf, b_wbf, t_wst, b_wst, tagp)
            for jc in range(8):
                hh = half * 8 + jc
                for tt in range(NTT):
                    par = (hh * NTT + tt) % 2
                    def e_proj(e, jc=jc, tt=tt, par=par):
                        for dc in range(NDC):
                            inst = e.matmul(
                                ps_p[par][:, :],
                                t_wbf[:, dc * NQ + jc * P: dc * NQ + jc * P + P],
                                ln_t[:, dc * NQ + tt * 512: dc * NQ + tt * 512 + 512],
                                start=(dc == 0), stop=(dc == NDC - 1))
                        return inst
                    pr.step("tensor", e_proj, reads=[b_wbf, ln_b], writes=[pb_p[par]],
                            inc=SP, tag=f"{tagp}mm{hh}_{tt}")
                    pr.step("vector", lambda e, par=par: e.tensor_copy(t_qraw[:], ps_p[par][:, :]),
                            reads=[pb_p[par]], writes=[b_qraw], inc=SV, settle=True,
                            tag=f"{tagp}ev{hh}_{tt}")
                    def e_swp(e):
                        return [e.dma_start(t_qswp[0:64, :], t_qraw[64:128, :]),
                                e.dma_start(t_qswp[64:128, :], t_qraw[0:64, :])]
                    pr.step("gpsimd", e_swp, reads=[b_qraw], writes=[b_qswp], inc=SG,
                            ninc=2, tag=f"{tagp}sw{hh}_{tt}")
                    pr.step("vector", lambda e, tt=tt: e.tensor_tensor(
                        t_t1[:], t_qraw[:], t_cq[:, tt * 512:(tt + 1) * 512], ALU.mult),
                        reads=[b_qraw, b_cq], writes=[b_t1], inc=SV,
                        tag=f"{tagp}c{hh}_{tt}")
                    pr.step("gpsimd", lambda e, tt=tt: e.tensor_tensor(
                        t_t2[:], t_qswp[:], t_sn[:, tt * 512:(tt + 1) * 512], ALU.mult),
                        reads=[b_qswp, b_sn], writes=[b_t2], inc=SL,
                        tag=f"{tagp}s{hh}_{tt}")
                    pr.step("vector", lambda e, hh=hh, tt=tt: e.tensor_tensor(
                        out_t[:, hh * NQ + tt * 512: hh * NQ + tt * 512 + 512],
                        t_t1[:], t_t2[:], ALU.add),
                        reads=[b_t1, b_t2], writes=[out_b], inc=SV,
                        tag=f"{tagp}a{hh}_{tt}")

    def s2_v(st, ln_t, ln_b, v_t, v_b):
        t_wst, b_wst = [None] * 2, [None] * 2
        for i in range(2):
            t_wst[i], b_wst[i] = sbuf(f"vwst{i}", [P, NQ], F32, st)
        t_wbf, b_wbf = sbuf("vwbf", [P, NDC * NQ], BF16, st)
        ps_p, pb_p = [None] * 2, [None] * 2
        for i in range(2):
            ps_p[i], pb_p[i] = psum(f"vps{i}", [P, 512], F32, st)
        for half in range(2):
            proj_weight_half(st, "WvT", half, t_wbf, b_wbf, t_wst, b_wst, "Wv")
            for kc in range(NKC):
                for jt in range(2):
                    par = (kc * 2 + jt) % 2
                    def e_vp(e, kc=kc, jt=jt, par=par):
                        for dc in range(NDC):
                            inst = e.matmul(
                                ps_p[par][:, :],
                                ln_t[:, dc * NQ + kc * P: dc * NQ + kc * P + P],
                                t_wbf[:, dc * NQ + jt * 512: dc * NQ + jt * 512 + 512],
                                start=(dc == 0), stop=(dc == NDC - 1))
                        return inst
                    pr.step("tensor", e_vp, reads=[b_wbf, ln_b], writes=[pb_p[par]],
                            inc=SP, tag=f"Vmm{half}_{kc}_{jt}")
                    jf0 = half * NQ + jt * 512
                    pr.step("vector", lambda e, kc=kc, jf0=jf0, par=par: e.tensor_copy(
                        v_t[:, kc * 2048 + jf0: kc * 2048 + jf0 + 512], ps_p[par][:, :]),
                        reads=[pb_p[par]], writes=[v_b], inc=SV,
                        tag=f"Vev{half}_{kc}_{jt}")

    def s3(st, t_QT, b_QT, t_KT, b_KT, t_V, b_V, t_AT, b_AT):
        t_ES, b_ES = sbuf("ExpS", [P, NKC * NQ], BF16, st)
        t_rs, b_rs = sbuf("rssb", [2, NQ], F32, st)
        t_rr, b_rr = sbuf("rrrow", [1, NQ], F32, st)
        t_rrb, b_rrb = sbuf("rrb", [P, NQ], F32, st)
        t_vrow, b_vrow = sbuf("vrow", [1, 2048], BF16, st)
        t_srow, b_srow = sbuf("srow", [1, 256], F32, st)
        t_srowb, b_srowb = sbuf("srowB", [1, 256], BF16, st)
        t_ve, b_ve = sbuf("verow", [1, P], BF16, st)
        t_p0, b_p0 = sbuf("pcr0", [1, H], F32, st)
        t_p1, b_p1 = sbuf("pcr1", [1, H], F32, st)
        t_p2, b_p2 = sbuf("pcr2", [1, H], F32, st)
        t_p3, b_p3 = sbuf("pcr3", [1, H], F32, st)
        t_ev, b_ev = sbuf("evrow", [1, H], F32, st)
        t_evb, b_evb = sbuf("evrowB", [1, H], BF16, st)
        ps_S, pb_S = [None] * 4, [None] * 4
        for i in range(4):
            ps_S[i], pb_S[i] = psum(f"psS{i}", [P, 512], F32, st)
        ps_o, pb_o = [None] * 2, [None] * 2
        for i in range(2):
            ps_o[i], pb_o[i] = psum(f"psO{i}", [P, 512], F32, st)
        ps_rs, pb_rs = [None] * 2, [None] * 2
        for i in range(2):
            ps_rs[i], pb_rs[i] = psum(f"psRS{i}", [2, 512], F32, st)

        pr.step("gpsimd", lambda e: e.dma_start(t_vrow[0:1, :],
                                                t_V[76:77, 1 * 2048:2 * 2048]),
                reads=[b_V], writes=[b_vrow], inc=SG, tag="vrow")

        for h in range(H):
            for kc in range(NKC):
                for tt in range(NTT):
                    par = (kc * NTT + tt) % 4
                    pr.step("tensor", lambda e, h=h, kc=kc, tt=tt, par=par: e.matmul(
                        ps_S[par][:, :],
                        t_KT[:, h * NQ + kc * P: h * NQ + kc * P + P],
                        t_QT[:, h * NQ + tt * 512: h * NQ + tt * 512 + 512],
                        start=True, stop=True),
                        reads=[b_KT, b_QT], writes=[pb_S[par]], inc=SP,
                        tag=f"sc{h}_{kc}_{tt}")
                    pr.step("scalar", lambda e, h=h, kc=kc, tt=tt, par=par: e.activation(
                        t_ES[:, kc * NQ + tt * 512: kc * NQ + tt * 512 + 512],
                        ps_S[par][:, :], AF.Exp, scale=cN["rsqd"][:, :]),
                        reads=[pb_S[par], cB["rsqd"]], writes=[b_ES], inc=SS,
                        tag=f"ex{h}_{kc}_{tt}")
            for tt in range(NTT):
                def e_rs(e, tt=tt):
                    for kc in range(NKC):
                        inst = e.matmul(ps_rs[tt][:, :], t_mask[:, 2 * kc:2 * kc + 2],
                                        t_ES[:, kc * NQ + tt * 512: kc * NQ + tt * 512 + 512],
                                        start=(kc == 0), stop=(kc == NKC - 1))
                    return inst
                pr.step("tensor", e_rs, reads=[b_ES, b_mask], writes=[pb_rs[tt]], inc=SP,
                        tag=f"rs{h}_{tt}")
            def e_rsev(e):
                e.tensor_copy(t_rs[:, 0:512], ps_rs[0][:, :])
                return e.tensor_copy(t_rs[:, 512:1024], ps_rs[1][:, :])
            pr.step("vector", e_rsev, reads=[pb_rs[0], pb_rs[1]], writes=[b_rs], inc=SV,
                    tag=f"rsev{h}", settle=True)
            for tt in range(NTT):
                def e_pv(e, h=h, tt=tt):
                    for kc in range(NKC):
                        inst = e.matmul(ps_o[tt][:, :],
                                        t_V[:, kc * 2048 + h * P: kc * 2048 + h * P + P],
                                        t_ES[:, kc * NQ + tt * 512: kc * NQ + tt * 512 + 512],
                                        start=(kc == 0),
                                        stop=(kc == NKC - 1 and tt == 0))
                    return inst
                pr.step("tensor", e_pv, reads=[b_ES, b_V], writes=[pb_o[tt]], inc=SP,
                        tag=f"pv{h}_{tt}")
            pr.step("gpsimd", lambda e, h=h: e.dma_start(
                t_evb[0:1, h:h + 1], t_ES[76:77, 1 * NQ + 1023: 1 * NQ + 1024]),
                reads=[b_ES], writes=[b_evb], inc=SG, tag=f"ev{h}")
            pr.step("vector", lambda e, h=h: e.tensor_copy(t_ev[0:1, h:h + 1],
                                                           t_evb[0:1, h:h + 1]),
                    reads=[b_evb], writes=[b_ev], inc=SV, tag=f"evc{h}", settle=True)
            def e_sc01(e, h=h):
                return [e.dma_start(t_p0[0:1, h:h + 1], t_rs[0:1, 1023:1024]),
                        e.dma_start(t_p1[0:1, h:h + 1], t_rs[1:2, 1023:1024])]
            pr.step("gpsimd", e_sc01, reads=[b_rs], writes=[b_p0, b_p1], inc=SG,
                    ninc=2, tag=f"sc01{h}")
            def e_den(e, h=h):
                e.tensor_scalar(t_p2[0:1, h:h + 1], t_p1[0:1, h:h + 1],
                                cN["r818"][0:1, 0:1], None, ALU.mult)
                return e.tensor_scalar(t_p3[0:1, h:h + 1], t_p0[0:1, h:h + 1],
                                       cN["c1e6"][0:1, 0:1], None, ALU.mult)
            pr.step("vector", e_den, reads=[b_p0, b_p1, cB["r818"], cB["c1e6"]],
                    writes=[b_p2, b_p3], inc=SV, tag=f"den{h}", settle=True)
            pr.step("vector", lambda e, h=h: e.tensor_tensor(
                t_p2[0:1, h:h + 1], t_p2[0:1, h:h + 1], t_p3[0:1, h:h + 1], ALU.add),
                reads=[b_p2, b_p3], writes=[b_p2], inc=SV, tag=f"den2{h}", settle=True)
            pr.step("vector", lambda e, h=h: e.reciprocal(t_p3[0:1, h:h + 1],
                                                          t_p2[0:1, h:h + 1]),
                    reads=[b_p2], writes=[b_p3], inc=SV, tag=f"rden{h}", settle=True)
            pr.step("vector", lambda e, h=h: e.tensor_tensor(
                t_p2[0:1, h:h + 1], t_ev[0:1, h:h + 1], t_p3[0:1, h:h + 1], ALU.mult),
                reads=[b_p3, b_ev], writes=[b_p2], inc=SV, tag=f"pm{h}", settle=True)
            pr.step("vector", lambda e, h=h: e.tensor_scalar(
                t_p2[0:1, h:h + 1], t_p2[0:1, h:h + 1], 1.0, 0.0, ALU.min, ALU.max),
                reads=[b_p2], writes=[b_p2], inc=SV, tag=f"p{h}", settle=True)
            pr.step("vector", lambda e, h=h: e.tensor_tensor(
                t_p3[0:1, h:h + 1], t_urow[0:1, h:h + 1], t_p2[0:1, h:h + 1], ALU.is_lt),
                reads=[b_p2, b_urow], writes=[b_p3], inc=SV, tag=f"bc{h}", settle=True)
            pr.step("vector", lambda e, h=h: e.tensor_scalar(
                t_p3[0:1, h:h + 1], t_p3[0:1, h:h + 1], cN["r819"][0:1, 0:1], None, ALU.mult),
                reads=[b_p3, cB["r819"]], writes=[b_p3], inc=SV, tag=f"bern{h}", settle=True)
            pr.step("vector", lambda e, h=h: e.tensor_scalar(
                t_ve[0:1, :], t_vrow[0:1, h * P:(h + 1) * P], t_p3[0:1, h:h + 1],
                None, ALU.mult),
                reads=[b_vrow, b_p3], writes=[b_ve], inc=SV, tag=f"ve{h}", settle=True)
            pr.step("gpsimd", lambda e: e.dma_start(t_srow[0:1, :], t_rs[1:2, 768:1024]),
                    reads=[b_rs], writes=[b_srow], inc=SG, tag=f"srow{h}")
            pr.step("vector", lambda e: e.tensor_copy(t_srowb[0:1, :], t_srow[0:1, :]),
                    reads=[b_srow], writes=[b_srowb], inc=SV, tag=f"srowb{h}", settle=True)
            pr.step("tensor", lambda e: e.matmul(ps_o[1][:, 256:512], t_ve[0:1, :],
                                                 t_srowb[0:1, :], start=False, stop=True),
                    reads=[b_ve, b_srowb], writes=[pb_o[1]], inc=SP, tag=f"corr{h}")
            pr.step("vector", lambda e: e.reciprocal(t_rr[0:1, :], t_rs[0:1, :]),
                    reads=[b_rs], writes=[b_rr], inc=SV, tag=f"rr{h}", settle=True)
            for tt in range(NTT):
                par = tt % 4
                pr.step("tensor", lambda e, tt=tt, par=par: e.matmul(
                    ps_S[par][:, :], t_onesr[:], t_rr[0:1, tt * 512:(tt + 1) * 512],
                    start=True, stop=True),
                    reads=[b_rr, b_onesr], writes=[pb_S[par]], inc=SP, tag=f"rrb{h}_{tt}")
                pr.step("vector", lambda e, tt=tt, par=par: e.tensor_copy(
                    t_rrb[:, tt * 512:(tt + 1) * 512], ps_S[par][:, :]),
                    reads=[pb_S[par]], writes=[b_rrb], inc=SV, tag=f"rrev{h}_{tt}")
                pr.step("vector", lambda e, h=h, tt=tt: e.tensor_tensor(
                    t_AT[:, h * NQ + tt * 512: h * NQ + tt * 512 + 512],
                    ps_o[tt][:, :], t_rrb[:, tt * 512:(tt + 1) * 512], ALU.mult),
                    reads=[pb_o[tt], b_rrb], writes=[b_AT], inc=SV, tag=f"oev{h}_{tt}")

    def s4_proj(st, t_AT, b_AT, t_abf, b_abf):
        t_wst, b_wst = [None] * 2, [None] * 2
        for i in range(2):
            t_wst[i], b_wst[i] = sbuf(f"o_wst{i}", [P, NQ], F32, st)
        t_wbf, b_wbf = sbuf("o_wbf", [P, NDC * NQ], BF16, st)
        ps_p, pb_p = [None] * 2, [None] * 2
        for i in range(2):
            ps_p[i], pb_p[i] = psum(f"o_ps{i}", [P, 512], F32, st)

        for half in range(2):
            for dc in range(NDC):
                par = dc % 2
                pr.step("sync", lambda e, dc=dc, half=half, par=par: e.dma_start(
                    t_wst[par][:], dW["WoT"][dc * P:(dc + 1) * P, half * NQ:(half + 1) * NQ]),
                    writes=[b_wst[par]], inc=SD, tag=f"Wold{half}_{dc}")
                pr.step("vector", lambda e, dc=dc, par=par: e.tensor_copy(
                    t_wbf[:, dc * NQ:(dc + 1) * NQ], t_wst[par][:]),
                    reads=[b_wst[par]], writes=[b_wbf], inc=SV, tag=f"Wocast{half}_{dc}")
            for jc in range(8):
                oc = half * 8 + jc
                for tt in range(NTT):
                    par = (oc * NTT + tt) % 2
                    def e_mm(e, jc=jc, tt=tt, par=par):
                        for dc in range(NDC):
                            inst = e.matmul(
                                ps_p[par][:, :],
                                t_wbf[:, dc * NQ + jc * P: dc * NQ + jc * P + P],
                                t_AT[:, dc * NQ + tt * 512: dc * NQ + tt * 512 + 512],
                                start=(dc == 0), stop=(dc == NDC - 1))
                        return inst
                    pr.step("tensor", e_mm, reads=[b_wbf, b_AT], writes=[pb_p[par]],
                            inc=SP, tag=f"Womm{oc}_{tt}")
                    pr.step("vector", lambda e, oc=oc, tt=tt, par=par: e.tensor_copy(
                        t_abf[:, oc * NQ + tt * 512: oc * NQ + tt * 512 + 512],
                        ps_p[par][:, :]),
                        reads=[pb_p[par]], writes=[b_abf], inc=SV, tag=f"Woev{oc}_{tt}")

    def ln_res(st, tag, t_in, b_in, res_dram, res_sbuf, t_dst, b_dst):
        t_sq, b_sq = [None] * 2, [None] * 2
        for i in range(2):
            t_sq[i], b_sq[i] = sbuf(tag + f"sq{i}", [P, 512], BF16, st)
        t_r0, b_r0 = sbuf(tag + "r0", [1, NQ], F32, st)
        t_r1, b_r1 = sbuf(tag + "r1", [1, NQ], F32, st)
        t_r2, b_r2 = sbuf(tag + "r2", [1, NQ], F32, st)
        t_r3, b_r3 = sbuf(tag + "r3", [1, NQ], F32, st)
        t_A, b_A = sbuf(tag + "A", [1, NQ], F32, st)
        t_Br, b_Br = sbuf(tag + "B", [1, NQ], F32, st)
        t_s1, b_s1 = sbuf(tag + "s1", [P, 4 * 512], F32, st)
        if res_dram is not None:
            t_xst, b_xst = [None] * 2, [None] * 2
            for i in range(2):
                t_xst[i], b_xst[i] = sbuf(tag + f"xst{i}", [P, 4 * 512], F32, st)
        ps_s, pb_s = psum(tag + "pss", [1, 512], F32, st)
        ps_q, pb_q = psum(tag + "psq", [1, 512], F32, st)
        ps_A, pb_A = psum(tag + "psA", [P, 512], F32, st)
        ps_B, pb_B = psum(tag + "psB", [P, 512], F32, st)
        for tt in range(NTT):
            for dc in range(NDC):
                par = dc % 2
                pr.step("scalar", lambda e, dc=dc, tt=tt, par=par: e.activation(
                    t_sq[par][:], t_in[:, dc * NQ + tt * 512: dc * NQ + tt * 512 + 512],
                    AF.Square),
                    reads=[b_in], writes=[b_sq[par]], inc=SS, tag=f"{tag}sq{tt}_{dc}")
                def e_mm(e, dc=dc, tt=tt, par=par):
                    e.matmul(ps_s[:, :], t_ones1[:],
                             t_in[:, dc * NQ + tt * 512: dc * NQ + tt * 512 + 512],
                             start=(dc == 0), stop=(dc == NDC - 1))
                    return e.matmul(ps_q[:, :], t_ones1[:], t_sq[par][:],
                                    start=(dc == 0), stop=(dc == NDC - 1))
                pr.step("tensor", e_mm, reads=[b_in, b_sq[par], b_ones1],
                        writes=([pb_s, pb_q] if dc == NDC - 1 else []), inc=SP,
                        tag=f"{tag}mm{tt}_{dc}")
            sl = slice(tt * 512, tt * 512 + 512)
            def e_rows(e, sl=sl):
                e.tensor_scalar(t_r0[0:1, sl], ps_s[:, :], 1.0 / 2048.0, None, ALU.mult)
                return e.tensor_scalar(t_r1[0:1, sl], ps_q[:, :], 1.0 / 2048.0, None, ALU.mult)
            pr.step("vector", e_rows, reads=[pb_s, pb_q], writes=[b_r0, b_r1], inc=SV,
                    tag=tag + "rows", settle=True)
            pr.step("scalar", lambda e, sl=sl: e.activation(t_r2[0:1, sl], t_r0[0:1, sl],
                                                            AF.Square),
                    reads=[b_r0], writes=[b_r2], inc=SS, tag=tag + "m2", settle=True)
            pr.step("vector", lambda e, sl=sl: e.tensor_sub(t_r2[0:1, sl], t_r1[0:1, sl],
                                                            t_r2[0:1, sl]),
                    reads=[b_r1, b_r2], writes=[b_r2], inc=SV, tag=tag + "var", settle=True)
            pr.step("scalar", lambda e, sl=sl: e.activation(t_r3[0:1, sl], t_r2[0:1, sl],
                                                            AF.Ln, bias=cN["ceps"][0:1, :]),
                    reads=[b_r2, cB["ceps"]], writes=[b_r3], inc=SS, tag=tag + "lnv", settle=True)
            pr.step("scalar", lambda e, sl=sl: e.activation(t_A[0:1, sl], t_r3[0:1, sl],
                                                            AF.Exp, scale=-0.5),
                    reads=[b_r3], writes=[b_A], inc=SS, tag=tag + "rstd", settle=True)
            pr.step("vector", lambda e, sl=sl: e.tensor_mul(t_Br[0:1, sl], t_r0[0:1, sl],
                                                            t_A[0:1, sl]),
                    reads=[b_r0, b_A], writes=[b_Br], inc=SV, tag=tag + "Br", settle=True)
            pr.step("tensor", lambda e, sl=sl: e.matmul(ps_A[:, :], t_onesr[:], t_A[0:1, sl],
                                                        start=True, stop=True),
                    reads=[b_A, b_onesr], writes=[pb_A], inc=SP, tag=tag + "bA")
            pr.step("tensor", lambda e, sl=sl: e.matmul(ps_B[:, :], t_onesr[:], t_Br[0:1, sl],
                                                        start=True, stop=True),
                    reads=[b_Br, b_onesr], writes=[pb_B], inc=SP, tag=tag + "bB")
            for half in range(4):
                dcs = list(range(half * 4, half * 4 + 4))
                if res_dram is not None:
                    def e_ld(e, dcs=dcs, tt=tt, half=half):
                        return [e.dma_start(
                            t_xst[half % 2][:, j * 512:(j + 1) * 512],
                            res_dram[dc * P:(dc + 1) * P, tt * 512: tt * 512 + 512])
                            for j, dc in enumerate(dcs)]
                    pr.step("sync", e_ld, writes=[b_xst[half % 2]], inc=SD, ninc=4,
                            tag=f"{tag}ld{tt}_{half}")
                def e_mul(e, dcs=dcs, tt=tt):
                    for j, dc in enumerate(dcs):
                        inst = e.tensor_tensor(
                            t_s1[:, j * 512:(j + 1) * 512],
                            t_in[:, dc * NQ + tt * 512: dc * NQ + tt * 512 + 512],
                            ps_A[:, :], ALU.mult)
                    return inst
                pr.step("vector", e_mul, reads=[b_in, pb_A], writes=[b_s1], inc=SV,
                        tag=f"{tag}ap1_{tt}_{half}")
                def e_sub(e, dcs=dcs, tt=tt):
                    for j, dc in enumerate(dcs):
                        inst = e.tensor_tensor(t_s1[:, j * 512:(j + 1) * 512],
                                               t_s1[:, j * 512:(j + 1) * 512],
                                               ps_B[:, :], ALU.subtract)
                    return inst
                pr.step("vector", e_sub, reads=[b_s1, pb_B], writes=[b_s1], inc=SV,
                        tag=f"{tag}ap2_{tt}_{half}")
                if res_dram is not None:
                    def e_add(e, dcs=dcs, tt=tt, half=half):
                        for j, dc in enumerate(dcs):
                            inst = e.tensor_tensor(
                                t_dst[:, dc * NQ + tt * 512: dc * NQ + tt * 512 + 512],
                                t_s1[:, j * 512:(j + 1) * 512],
                                t_xst[half % 2][:, j * 512:(j + 1) * 512], ALU.add)
                        return inst
                    pr.step("vector", e_add, reads=[b_s1, b_xst[half % 2]], writes=[b_dst],
                            inc=SV, tag=f"{tag}ap3_{tt}_{half}")
                else:
                    t_res, b_res = res_sbuf
                    def e_add(e, dcs=dcs, tt=tt):
                        for j, dc in enumerate(dcs):
                            inst = e.tensor_tensor(
                                t_dst[:, dc * NQ + tt * 512: dc * NQ + tt * 512 + 512],
                                t_s1[:, j * 512:(j + 1) * 512],
                                t_res[:, dc * NQ + tt * 512: dc * NQ + tt * 512 + 512],
                                ALU.add)
                        return inst
                    rd = [b_s1] + ([b_res] if b_res is not b_dst else [])
                    pr.step("vector", e_add, reads=rd, writes=[b_dst],
                            inc=SV, tag=f"{tag}ap3_{tt}")

    d_eyeb = nc.dram_tensor("eyeb", [P, P], BF16, kind="ExternalInput")

    def s5(st, t_x1, b_x1, t_xdeqT, b_xdeqT):
        t_eyB, b_eyB = sbuf("eyeBs", [P, P], BF16, st)
        pr.step("sync", lambda e: e.dma_start(t_eyB[:], d_eyeb[:, :]), writes=[b_eyB],
                inc=SD, tag="eyebld")
        t_xtok, b_xtok = sbuf("xtok", [P, 2048], F32, st)
        t_tc, b_tc = sbuf("ttok", [P, 2048], F32, st)
        t_sqf, b_sqf = sbuf("sqful", [P, 2048], F32, st)
        t_op, b_op = sbuf("outp", [P, 2048], F32, st)
        t_xdb, b_xdb = sbuf("xdtok", [P, 2048], BF16, st)
        t_st, b_st = sbuf("tstat", [P, 4], F32, st)
        t_am, b_am = sbuf("tamax", [P, 16], F32, st)
        t_rc, b_rc = sbuf("trc", [P, 16], F32, st)
        t_s0, b_s0 = sbuf("ts0", [P, 16], F32, st)
        ps_tr, pb_tr = [None] * 2, [None] * 2
        for i in range(2):
            ps_tr[i], pb_tr[i] = psum(f"pstr{i}", [P, P], F32, st)
        ps_tb, pb_tb = [None] * 2, [None] * 2
        for i in range(2):
            ps_tb[i], pb_tb[i] = psum(f"pstb{i}", [P, P], BF16, st)
        for tc in range(8):
            for dc in range(NDC):
                par = dc % 2
                pr.step("tensor", lambda e, dc=dc, tc=tc, par=par: e.transpose(
                    ps_tr[par][:, :], t_x1[:, dc * NQ + tc * P: dc * NQ + tc * P + P],
                    t_eye[:]),
                    reads=[b_x1, b_eye], writes=[pb_tr[par]], inc=SP, tag=f"tr{tc}_{dc}")
                pr.step("vector", lambda e, dc=dc, par=par: e.tensor_copy(
                    t_xtok[:, dc * P:(dc + 1) * P], ps_tr[par][:, :]),
                    reads=[pb_tr[par]], writes=[b_xtok], inc=SV, tag=f"trev{tc}_{dc}")
            pr.step("vector", lambda e: e.tensor_reduce(t_st[:, 0:1], t_xtok[:], AX.X, ALU.add),
                    reads=[b_xtok], writes=[b_st], inc=SV, tag=f"tsum{tc}", settle=True)
            pr.step("vector", lambda e: e.tensor_scalar(t_st[:, 3:4], t_st[:, 0:1],
                                                        -1.0 / 2048.0, None, ALU.mult),
                    reads=[b_st], writes=[b_st], inc=SV, tag=f"tnm{tc}", settle=True)
            pr.step("scalar", lambda e: e.activation(t_tc[:], t_xtok[:], AF.Identity,
                                                     bias=t_st[:, 3:4]),
                    reads=[b_xtok, b_st], writes=[b_tc], inc=SS, tag=f"tshift{tc}")
            pr.step("scalar", lambda e: e.activation(t_sqf[:], t_tc[:], AF.Square),
                    reads=[b_tc], writes=[b_sqf], inc=SS, tag=f"tsq{tc}")
            pr.step("vector", lambda e: e.tensor_reduce(t_st[:, 1:2], t_sqf[:], AX.X, ALU.add),
                    reads=[b_sqf], writes=[b_st], inc=SV, tag=f"tvar{tc}", settle=True)
            def e_varc(e):
                return e.tensor_scalar(t_st[:, 1:2], t_st[:, 1:2], 1.0 / 2048.0, None, ALU.mult)
            pr.step("vector", e_varc, reads=[b_st], writes=[b_st], inc=SV, tag=f"tvc{tc}", settle=True)
            pr.step("scalar", lambda e: e.activation(t_st[:, 2:3], t_st[:, 1:2], AF.Ln,
                                                     bias=cN["ceps"][:, 0:1]),
                    reads=[b_st, cB["ceps"]], writes=[b_st], inc=SS, tag=f"tlv{tc}", settle=True)
            pr.step("scalar", lambda e: e.activation(t_st[:, 2:3], t_st[:, 2:3], AF.Exp,
                                                     scale=-0.5),
                    reads=[b_st], writes=[b_st], inc=SS, tag=f"trs{tc}", settle=True)
            pr.step("vector", lambda e: e.tensor_reduce(
                t_am[:, :], t_tc[:].rearrange("p (c f) -> p c f", f=P), AX.X, ALU.max,
                apply_absolute_value=True),
                reads=[b_tc], writes=[b_am], inc=SV, tag=f"tam{tc}", settle=True)
            pr.step("vector", lambda e: e.reciprocal(t_rc[:, :], t_am[:, :]),
                    reads=[b_am], writes=[b_rc], inc=SV, tag=f"trc{tc}", settle=True)
            def e_rs(e):
                e.tensor_scalar(t_rc[:, :], t_rc[:, :], 448.0, None, ALU.mult)
                return e.tensor_scalar(t_s0[:, :], t_am[:, :],
                                       cN["r448"][:, 0:1], None, ALU.mult)
            pr.step("vector", e_rs, reads=[b_rc, b_am, cB["r448"]],
                    writes=[b_rc, b_s0], inc=SV, tag=f"trs0{tc}", settle=True)
            qtile(t_tc, b_tc, t_s0, b_s0, t_rc, b_rc, t_op, b_op)
            pr.step("scalar", lambda e: e.activation(t_xdb[:], t_op[:], AF.Identity,
                                                     scale=t_st[:, 2:3]),
                    reads=[b_op, b_st], writes=[b_xdb], inc=SS, tag=f"txc{tc}")
            for dc in range(NDC):
                par = dc % 2
                pr.step("tensor", lambda e, dc=dc, par=par: e.transpose(
                    ps_tb[par][:, :], t_xdb[:, dc * P:(dc + 1) * P], t_eyB[:]),
                    reads=[b_xdb, b_eyB], writes=[pb_tb[par]], inc=SP, tag=f"trb{tc}_{dc}")
                pr.step("vector", lambda e, dc=dc, tc=tc, par=par: e.tensor_copy(
                    t_xdeqT[:, dc * NQ + tc * P: dc * NQ + tc * P + P], ps_tb[par][:, :]),
                    reads=[pb_tb[par]], writes=[b_xdeqT], inc=SV, tag=f"trbe{tc}_{dc}")

    _q = {}
    _q_stack = [es]

    def qtile(t_x, b_x, t_s0col, b_s0col, t_rc, b_rc, t_out, b_out):
        if not _q:
            for nm in ["v", "l", "sp", "rp", "t1", "t2"]:
                _q[nm] = sbuf("q_" + nm, [P, 2048], F32, _q_stack[0])
        t_v, b_v = _q["v"]
        t_l, b_l = _q["l"]
        t_sp, b_sp = _q["sp"]
        t_rp, b_rp = _q["rp"]
        t_t1, b_t1 = _q["t1"]
        t_t2, b_t2 = _q["t2"]
        def e_v(e):
            for c in range(16):
                inst = e.tensor_scalar(t_v[:, c * P:(c + 1) * P], t_x[:, c * P:(c + 1) * P],
                                       t_rc[:, c:c + 1], None, ALU.mult)
            return inst
        pr.step("vector", e_v, reads=[b_x, b_rc], writes=[b_v], inc=SV, tag="qv")
        pr.step("scalar", lambda e: e.activation(t_l[:], t_v[:], AF.Square),
                reads=[b_v], writes=[b_l], inc=SS, tag="qsq")
        pr.step("scalar", lambda e: e.activation(t_l[:], t_l[:], AF.Ln, bias=cN["b35"][:, 0:1]),
                reads=[b_l, cB["b35"]], writes=[b_l], inc=SS, tag="qln")
        pr.step("vector", lambda e: e._custom_dve(ops["ANT_QEXP_A"], out=t_t1[:], in0=t_l[:],
                                                  in1=cN["c4"][:, 0:1], s0=cN["ccol"][:, 0:1],
                                                  s1=6.0, imm2=8.0),
                reads=[b_l, cB["c4"], cB["ccol"]], writes=[b_t1], inc=SV, tag="qa")
        pr.step("vector", lambda e: e._custom_dve(ops["ANT_QEXP_B"], out=t_t2[:], in0=t_t1[:],
                                                  in1=t_l[:], s0=cN["ccol"][:, 0:1],
                                                  s1=6.0, imm2=2.0),
                reads=[b_t1, b_l, cB["ccol"]], writes=[b_t2], inc=SV, tag="qb")
        pr.step("scalar", lambda e: e.activation(t_sp[:], t_t2[:], AF.Exp,
                                                 bias=cN["bm9"][:, :], scale=cN["ln2c"][:, :]),
                reads=[b_t2, cB["bm9"], cB["ln2c"]], writes=[b_sp], inc=SS, tag="qst")
        pr.step("scalar", lambda e: e.activation(t_rp[:], t_t2[:], AF.Exp,
                                                 bias=cN["bp9"][:, :], scale=cN["sln"][:, :]),
                reads=[b_t2, cB["bp9"], cB["sln"]], writes=[b_rp], inc=SS, tag="qrt")
        pr.step("vector", lambda e: e._custom_dve(ops["ANT_QLAD_A"], out=t_t1[:], in0=t_v[:],
                                                  in1=t_rp[:], s0=0.5, s1=0.0, imm2=16.0),
                reads=[b_v, b_rp], writes=[b_t1], inc=SV, tag="qla")
        pr.step("vector", lambda e: e._custom_dve(ops["ANT_QLAD_B"], out=t_t2[:], in0=t_t1[:],
                                                  s0=8.0, s1=4.0, imm2=0.0),
                reads=[b_t1], writes=[b_t2], inc=SV, tag="qlb")
        pr.step("vector", lambda e: e._custom_dve(ops["ANT_QLAD_C"], out=t_t1[:], in0=t_t2[:],
                                                  in1=t_sp[:], s0=0.5, s1=2.0, imm2=0.0),
                reads=[b_t2, b_sp], writes=[b_t1], inc=SV, tag="qlc")
        def e_dq(e):
            for c in range(16):
                inst = e._custom_dve(ops["ANT_QDEQ"], out=t_out[:, c * P:(c + 1) * P],
                                     in0=t_t1[:, c * P:(c + 1) * P],
                                     in1=t_x[:, c * P:(c + 1) * P],
                                     s0=t_s0col[:, c:c + 1], s1=0.0, imm2=0.0)
            return inst
        pr.step("vector", e_dq, reads=[b_t1, b_x, b_s0col], writes=[b_out], inc=SV,
                tag="qdq")

    def s6(st):
        t_eyB2, b_eyB2 = sbuf("eyeB6", [P, P], BF16, st)
        pr.step("sync", lambda e: e.dma_start(t_eyB2[:], d_eyeb[:, :]), writes=[b_eyB2],
                inc=SD, tag="eyebld6")
        t_wf, b_wf = [None] * 2, [None] * 2
        for i in range(2):
            t_wf[i], b_wf[i] = sbuf(f"wqst{i}", [P, 2048], F32, st)
        t_wo, b_wo = sbuf("wqout", [P, 2048], F32, st)
        t_wb, b_wb = sbuf("wqbf", [P, 2048], BF16, st)
        t_wam, b_wam = sbuf("wam", [P, 16], F32, st)
        t_wrc, b_wrc = sbuf("wrc", [P, 16], F32, st)
        t_ws0, b_ws0 = sbuf("ws0", [P, 16], F32, st)
        t_wev, b_wev = sbuf("wev", [P, P], BF16, st)
        ps_tb, pb_tb = [None] * 2, [None] * 2
        for i in range(2):
            ps_tb[i], pb_tb[i] = psum(f"ps6tb{i}", [P, P], BF16, st)
        for wt in range(16):
            par = wt % 2
            pr.step("sync", lambda e, wt=wt, par=par: e.dma_start(
                t_wf[par][:], dW["Wf"][wt * P:(wt + 1) * P, :]),
                writes=[b_wf[par]], inc=SD, tag=f"wql{wt}")
            pr.step("vector", lambda e, par=par: e.tensor_reduce(
                t_wam[:, :], t_wf[par][:].rearrange("p (c f) -> p c f", f=P), AX.X,
                ALU.max, apply_absolute_value=True),
                reads=[b_wf[par]], writes=[b_wam], inc=SV, tag=f"wam{wt}", settle=True)
            pr.step("vector", lambda e: e.reciprocal(t_wrc[:, :], t_wam[:, :]),
                    reads=[b_wam], writes=[b_wrc], inc=SV, tag=f"wrc{wt}", settle=True)
            def e_rs(e):
                e.tensor_scalar(t_wrc[:, :], t_wrc[:, :], 448.0, None, ALU.mult)
                return e.tensor_scalar(t_ws0[:, :], t_wam[:, :], cN["r448"][:, 0:1],
                                       None, ALU.mult)
            pr.step("vector", e_rs, reads=[b_wrc, b_wam, cB["r448"]],
                    writes=[b_wrc, b_ws0], inc=SV, tag=f"wrs{wt}", settle=True)
            qtile(t_wf[par], b_wf[par], t_ws0, b_ws0, t_wrc, b_wrc, t_wo, b_wo)
            pr.step("vector", lambda e: e.tensor_copy(t_wb[:], t_wo[:]),
                    reads=[b_wo], writes=[b_wb], inc=SV, tag=f"wcst{wt}")
            for dc in range(NDC):
                pp = dc % 2
                pr.step("tensor", lambda e, dc=dc, pp=pp: e.transpose(
                    ps_tb[pp][:, :], t_wb[:, dc * P:(dc + 1) * P], t_eyB2[:]),
                    reads=[b_wb, b_eyB2], writes=[pb_tb[pp]], inc=SP, tag=f"wtr{wt}_{dc}")
                pr.step("vector", lambda e, pp=pp: e.tensor_copy(t_wev[:], ps_tb[pp][:, :]),
                        reads=[pb_tb[pp]], writes=[b_wev], inc=SV, settle=True,
                        tag=f"wte{wt}_{dc}")
                pr.step("sync", lambda e, wt=wt, dc=dc: e.dma_start(
                    d_wscr[dc * P:(dc + 1) * P, wt * P:(wt + 1) * P], t_wev[:]),
                    reads=[b_wev], inc=SD, tag=f"wto{wt}_{dc}")

    def s7(st, t_x1, b_x1, t_xdeqT, b_xdeqT):
        t_wst, b_wst = [None] * 2, [None] * 2
        for i in range(2):
            t_wst[i], b_wst[i] = sbuf(f"f_wst{i}", [P, NQ], BF16, st)
        t_wbf, b_wbf = sbuf("f_wbf", [P, NDC * NQ], BF16, st)
        t_fbf, b_fbf = sbuf("fbf", [P, NDC * NQ], BF16, st)
        ps_p, pb_p = [None] * 2, [None] * 2
        for i in range(2):
            ps_p[i], pb_p[i] = psum(f"f_ps{i}", [P, 512], F32, st)
        for half in range(2):
            for dc in range(NDC):
                par = dc % 2
                pr.step("sync", lambda e, dc=dc, half=half, par=par: e.dma_start(
                    t_wst[par][:], d_wscr[dc * P:(dc + 1) * P, half * NQ:(half + 1) * NQ]),
                    writes=[b_wst[par]], inc=SD, tag=f"fwld{half}_{dc}")
                pr.step("vector", lambda e, dc=dc, par=par: e.tensor_copy(
                    t_wbf[:, dc * NQ:(dc + 1) * NQ], t_wst[par][:]),
                    reads=[b_wst[par]], writes=[b_wbf], inc=SV, tag=f"fwcp{half}_{dc}")
            for jc in range(8):
                oc = half * 8 + jc
                for tt in range(NTT):
                    par = (oc * NTT + tt) % 2
                    def e_mm(e, jc=jc, tt=tt, par=par):
                        for dc in range(NDC):
                            inst = e.matmul(
                                ps_p[par][:, :],
                                t_wbf[:, dc * NQ + jc * P: dc * NQ + jc * P + P],
                                t_xdeqT[:, dc * NQ + tt * 512: dc * NQ + tt * 512 + 512],
                                start=(dc == 0), stop=(dc == NDC - 1))
                        return inst
                    pr.step("tensor", e_mm, reads=[b_wbf, b_xdeqT], writes=[pb_p[par]],
                            inc=SP, tag=f"fmm{oc}_{tt}")
                    pr.step("vector", lambda e, oc=oc, tt=tt, par=par: e.tensor_copy(
                        t_fbf[:, oc * NQ + tt * 512: oc * NQ + tt * 512 + 512],
                        ps_p[par][:, :]),
                        reads=[pb_p[par]], writes=[b_fbf], inc=SV, tag=f"fev{oc}_{tt}")
        ln_res(st, "lnf", t_fbf, b_fbf, None, (t_x1, b_x1), t_x1, b_x1)
        def e_st(e):
            return [e.dma_start(d_out[dc * P:(dc + 1) * P, :],
                                t_x1[:, dc * NQ:(dc + 1) * NQ]) for dc in range(NDC)]
        pr.step("sync", e_st, reads=[b_x1], inc=SD, ninc=NDC, tag="outst")

    from contextlib import ExitStack as _ES
    ALLC = [SV, SS, SP, SD, SG, SL]
    s0()
    want = {"s1": 1, "s2": 2, "s3": 3, "s4": 4, "s5": 5, "s6": 6, "all": 9}[upto]

    st_qkv = _ES()
    st_ln1k = _ES()
    t_ln1k, b_ln1k = sbuf("ln1k", [P, NDC * NQ], BF16, st_ln1k, side="left")
    st_ln1q = _ES()
    t_ln1q, b_ln1q = sbuf("ln1q", [P, NDC * NQ], BF16, st_ln1q, side="left")
    _open = [st_ln1q, st_ln1k]

    def _close(stk):
        if stk in _open:
            _open.remove(stk)
            stk.close()
    with _ES() as st1:
        s1_set("l1q", d_xT, t_ln1q, b_ln1q, st1)
    pr.barrier(ALLC)
    with _ES() as st1b:
        s1_set("l1k", d_xkT, t_ln1k, b_ln1k, st1b)
    pr.barrier(ALLC)
    tap("ln1q", t_ln1q, b_ln1q)
    tap("ln1k", t_ln1k, b_ln1k)
    if want >= 2:
        t_QT, b_QT = sbuf("QT", [P, H * NQ], BF16, st_qkv)
        with _ES() as st2:
            s2_qk(st2, "WqT", "q", d_cosq, d_sinq, t_ln1q, b_ln1q, t_QT, b_QT)
        pr.barrier(ALLC)
        _close(st_ln1q)
        t_KT, b_KT = sbuf("KT", [P, H * NQ], BF16, st_qkv)
        with _ES() as st2b:
            s2_qk(st2b, "WkT", "k", d_cosk, d_sink, t_ln1k, b_ln1k, t_KT, b_KT)
        pr.barrier(ALLC)
        t_V, b_V = sbuf("Vtok", [P, NKC * 2048], BF16, st_qkv)
        with _ES() as st2c:
            s2_v(st2c, t_ln1k, b_ln1k, t_V, b_V)
        pr.barrier(ALLC)
        _close(st_ln1k)
        tap("QT", t_QT, b_QT)
        tap("KT", t_KT, b_KT)
        tap("V", t_V, b_V)
    st_at = _ES()
    _open.insert(0, st_at)
    if want >= 3:
        _close(st_ln1q)
        _close(st_ln1k)
        t_AT, b_AT = sbuf("AT", [P, H * NQ], BF16, st_at, side="left")
        with _ES() as st3:
            s3(st3, t_QT, b_QT, t_KT, b_KT, t_V, b_V, t_AT, b_AT)
        pr.barrier(ALLC)
        st_qkv.close()
        tap("AT", t_AT, b_AT)
    else:
        st_qkv.close()
    if want >= 4:
        st_abf = _ES()
        t_abf, b_abf = sbuf("abf", [P, NDC * NQ], BF16, st_abf)
        with _ES() as st4:
            s4_proj(st4, t_AT, b_AT, t_abf, b_abf)
        pr.barrier(ALLC)
        _close(st_at)
        t_x1, b_x1 = sbuf("x1T", [P, NDC * NQ], F32, side="left")
        with _ES() as st4b:
            ln_res(st4b, "lna", t_abf, b_abf, d_xT, None, t_x1, b_x1)
        pr.barrier(ALLC)
        st_abf.close()
        tap("x1", t_x1, b_x1)
    else:
        _close(st_at)
    if want >= 5:
        t_xdeqT, b_xdeqT = sbuf("xdeqT", [P, NDC * NQ], BF16, side="left")
        st56 = _ES()
        _q_stack[0] = st56
        for nm in ["v", "l", "sp", "rp", "t1", "t2"]:
            _q[nm] = sbuf("q_" + nm, [P, 2048], F32, st56)
        with _ES() as st5:
            s5(st5, t_x1, b_x1, t_xdeqT, b_xdeqT)
        pr.barrier(ALLC)
        tap("xdeqT", t_xdeqT, b_xdeqT)
    if want >= 6:
        with _ES() as st6:
            s6(st6)
        pr.barrier(ALLC)
    if want >= 9:
        st56.close()
        with _ES() as st7:
            s7(st7, t_x1, b_x1, t_xdeqT, b_xdeqT)
        tap("out", t_x1, b_x1)
    elif want >= 5:
        st56.close()
    for stk in list(_open):
        _close(stk)

    fin_sd = SD.n
    fin = {"vector": SV.n, "scalar": SS.n, "tensor": SP.n, "gpsimd": SG.n}
    with es:
        blk = es.enter_context(nc.Block())

        @blk.sync
        def _(sync):
            pr.emit("sync", sync)
            sync.wait_ge(SD.sem, fin_sd)
            for cn, (ctr, nm) in [("v", (SV, "vector")), ("s", (SS, "scalar")),
                                  ("t", (SP, "tensor")), ("g", (SG, "gpsimd")),
                                  ("l", (SL, "gpsimd"))]:
                if ctr.n:
                    sync.wait_ge(ctr.sem, ctr.n)

        @blk.vector
        def _(vector):
            pr.emit("vector", vector)

        @blk.scalar
        def _(scalar):
            pr.emit("scalar", scalar)

        @blk.tensor
        def _(tensor):
            pr.emit("tensor", tensor)

        @blk.gpsimd
        def _(gpsimd):
            pr.emit("gpsimd", gpsimd)

    mybir.codegen_inst_isa_subclasses(nc)
    return nc, dbg_out


HH, DD, DM, TSEQ = 16, 128, 2048, 2048
SB, RB = 204, 819
EVICT = TSEQ - RB
KEPT_IDX = np.concatenate([np.arange(SB), np.arange(EVICT, TSEQ)])
NQ = 1024
BF = ml_dtypes.bfloat16


def make_mask():
    m = np.zeros((128, 16), np.float32)
    for kc in range(8):
        ones = np.ones(128, np.float32)
        if kc == 7:
            ones[127] = 0.0
        ind = np.zeros(128, np.float32)
        if kc == 1:
            ind[77:] = 1.0
        elif kc >= 2:
            ind[:] = 1.0
        if kc == 7:
            ind[127] = 0.0
        m[:, 2 * kc] = ones
        m[:, 2 * kc + 1] = ind
    return m.astype(BF)


def prep_core(core, x, Wq, Wk, Wv, Wo, Wf, cos32, sin32, u_ref):
    b, half = core // 2, core % 2
    q0 = NQ * half
    sgn = np.ones((DD, 1), np.float32)
    sgn[:64] = -1.0
    qpos = np.arange(q0, q0 + NQ)
    kpos = np.concatenate([KEPT_IDX, [0]])
    xk = x[b][KEPT_IDX]
    xk_pad = np.concatenate([xk, np.zeros((1, DM), np.float32)], 0)
    m = {
        "xT": np.ascontiguousarray(x[b, q0:q0 + NQ].T),
        "xkT": np.ascontiguousarray(xk_pad.T),
        "WqT": np.ascontiguousarray(Wq.T),
        "WkT": np.ascontiguousarray(Wk.T),
        "WvT": np.ascontiguousarray(Wv.T),
        "WoT": np.ascontiguousarray(Wo.T),
        "Wf": np.ascontiguousarray(Wf),
        "cosq": np.ascontiguousarray(cos32[qpos].T),
        "sinq": np.ascontiguousarray(sin32[qpos].T * sgn),
        "cosk": np.ascontiguousarray(cos32[kpos].T),
        "sink": np.ascontiguousarray(sin32[kpos].T * sgn),
        "mask": make_mask(),
        "urow": (u_ref[b:b + 1].astype(np.float32) if half == 1
                 else np.full((1, HH), 1e30, np.float32)),
        "eyef": np.eye(128, dtype=np.float32),
        "eyeb": np.eye(128, dtype=np.float32).astype(BF),
    }
    return m


def featmat(dev):
    return np.concatenate([dev[:, dc * NQ:(dc + 1) * NQ] for dc in range(16)], axis=0)


_CACHE = {}


def kernel(L0_attn_hidden_states, L0_attn_Wq, L0_attn_Wk, L0_attn_Wv, L0_attn_Wo,
           L0_ffn_weight):
    x = np.ascontiguousarray(L0_attn_hidden_states, dtype=np.float32)
    cos32, sin32, u_ref = _make_tables()
    if "nc" not in _CACHE:
        _CACHE["nc"] = build()[0]
    nc = _CACHE["nc"]
    in_maps = [prep_core(c, x, L0_attn_Wq, L0_attn_Wk, L0_attn_Wv, L0_attn_Wo,
                         L0_ffn_weight, cos32, sin32, u_ref) for c in range(8)]
    res = run_bass_kernel_spmd(nc, in_maps, core_ids=list(range(8)))
    out = np.empty((4, 2048, 2048), np.float32)
    for c in range(8):
        b, half = c // 2, c % 2
        out[b, 1024 * half:1024 * half + 1024] = res.results[c]["outT"].T
    return out


# version 17 (final)
# speedup vs baseline: 1.1771x; 1.1771x over previous
import math
import numpy as np
import ml_dtypes
import concourse.bass as bass
import concourse.mybir as mybir
from concourse.bass_utils import run_bass_kernel_spmd
from concourse.dve_op import DveOp
from concourse.dve_spec import (
    Spec, Src0, Src1, C0, C1, C2, C3, Zero, One, maxx, lower,
)
from concourse.dve_table_gen import dve_ver_for
import concourse.dve_ops as dve_ops_mod

F32 = mybir.dt.float32
BF16 = mybir.dt.bfloat16
AF = mybir.ActivationFunctionType
ALU = mybir.AluOpType
AX = mybir.AxisListType
BF = ml_dtypes.bfloat16


def _make_tables():
    import jax
    jax.config.update("jax_enable_x64", True)
    import jax.numpy as jnp
    with jax.default_device(jax.devices("cpu")[0]):
        inv_freq = 1.0 / 10000.0 ** (jnp.arange(0, 128, 2, dtype=jnp.float32) / 128)
        freqs = jnp.arange(2048, dtype=jnp.float32)[:, None] * inv_freq[None, :]
        emb = jnp.concatenate([freqs, freqs], axis=-1)
        cos = np.asarray(jnp.cos(emb), np.float32)
        sin = np.asarray(jnp.sin(emb), np.float32)
        u = np.asarray(jax.random.uniform(
            jax.random.key(42, impl="threefry2x32"), (4, 16), jnp.float64))
    return cos, sin, u


class Ctr:
    def __init__(self, sem, name="", step=1):
        self.sem = sem
        self.n = 0
        self.name = name
        self.step = step


class Buf:
    __slots__ = ("name", "last_write", "readers")

    def __init__(self, name):
        self.name = name
        self.last_write = None
        self.readers = []


class Prog:
    def __init__(self, settle_fns=None, settle_bufs=None):
        self.steps = []
        self.floor = {}
        self.settle_fns = settle_fns or {}
        self.settle_bufs = settle_bufs or {}

    def barrier(self, ctrs):
        for c in ctrs:
            if c.n > 0:
                self.floor[c] = c.n

    def step(self, engine, emit, reads=(), writes=(), extra_waits=(), inc=None,
             ninc=1, settle=False, tag=""):
        if settle and engine in self.settle_bufs:
            jin, jout = self.settle_bufs[engine]
            reads = list(reads) + [jin]
            writes = list(writes) + [jout]
        waits = {}

        def add(m):
            if m is None:
                return
            c, v = m
            if c.step == 16:
                v = c.n
            if v > 0:
                waits[c] = max(waits.get(c, 0), v)
        for b in reads:
            add(b.last_write)
        for b in writes:
            add(b.last_write)
            for m in b.readers:
                add(m)
        for m in extra_waits:
            add(m)
        for c, v in self.floor.items():
            if v > 0:
                waits[c] = max(waits.get(c, 0), v)
        if inc is not None:
            if inc.step == 16 and inc.n > 0:
                waits[inc] = max(waits.get(inc, 0), inc.n)
            inc.n += inc.step * ninc
            marker = (inc, inc.n)
        else:
            marker = None
        self.steps.append((engine, emit, tuple(waits.items()), inc, ninc,
                           settle, tag))
        if marker is not None:
            for b in writes:
                b.last_write = marker
                b.readers = []
            for b in reads:
                b.readers.append(marker)
        else:
            assert not writes, f"step {tag}: writes declared but no inc"
            assert not reads, f"step {tag}: reads declared but no inc (WAR untrackable)"
        return marker

    def emit(self, engine_name, eng):
        n = 0
        seen = {}
        for engine, emit, waits, inc, ninc, settle, tag in self.steps:
            if engine != engine_name:
                continue
            for c, v in waits:
                if v > 0 and seen.get(c, -1) < v:
                    eng.wait_ge(c.sem, v)
                    seen[c] = v
            res = emit(eng)
            if settle and inc is not None:
                assert ninc == 1, f"step {tag}: settle with ninc>1"
                res = self.settle_fns[engine_name](eng)
            if inc is not None:
                insts = list(res) if isinstance(res, (list, tuple)) else [res]
                assert len(insts) == ninc and all(i is not None for i in insts), (
                    f"step {tag}: ninc={ninc} but emit returned {len(insts)}")
                for i in insts:
                    i.then_inc(inc.sem, inc.step)
            n += 1
        return n


_H2 = One + One
_H4 = _H2 * _H2
_H8 = _H4 * _H2
_H16 = _H4 * _H4


def _ref_qexp_a(in0, in1, c0, c1, c2):
    a = in0 * c0 + c1
    r8 = (a >= 8.0).astype(a.dtype)
    rem = a - r8 * 8.0
    r4 = (rem >= 4.0).astype(a.dtype)
    return rem - r4 * 4.0


def _ref_qexp_b(in0, in1, c0, c1, c2):
    rem = in0
    r2 = (rem >= 2.0).astype(rem.dtype)
    rem = rem - r2 * 2.0
    r1 = (rem >= 1.0).astype(rem.dtype)
    rem4 = rem - r1
    return (in1 * c0 + c1) - rem4


def _ref_qlad_a(in0, in1, c0, c1, c2):
    w = in0 * in1
    aw = np.maximum(w, -w)
    t = aw + c0
    r16 = (t >= 16.0).astype(t.dtype)
    return t - r16 * 16.0


def _ref_qlad_b(in0, in1, c0, c1, c2):
    rem = in0
    r8 = (rem >= 8.0).astype(rem.dtype)
    rem = rem - r8 * 8.0
    r4 = (rem >= 4.0).astype(rem.dtype)
    return rem - r4 * 4.0


def _ref_qlad_c(in0, in1, c0, c1, c2):
    rem = in0
    r2 = (rem >= 2.0).astype(rem.dtype)
    rem = rem - r2 * 2.0
    r1 = (rem >= 1.0).astype(rem.dtype)
    rem = rem - r1
    return (c0 - rem) * in1


def _ref_qdeq(in0, in1, c0, c1, c2):
    ds = in0 * c0
    sg = (in1 >= 0.0).astype(in0.dtype)
    sg3 = (sg + sg) - 1.0
    return in1 + sg3 * ds


def _spec_qexp_a():
    a = Src0 * C0 + C1
    r8 = a >= C2
    rem = a - r8 * C2
    r4 = rem >= C3
    return Spec(body=rem - r4 * C3)


def _spec_qexp_b():
    r2 = Src0 >= C2
    rem = Src0 - r2 * C2
    r1 = rem >= One
    rem4 = rem - r1
    return Spec(body=(Src1 * C0 + C1) - rem4)


def _spec_qlad_a():
    w = Src0 * Src1
    aw = maxx(w, Zero - w)
    t = aw + C0
    r16 = t >= C2
    return Spec(body=t - r16 * C2)


def _spec_qlad_b():
    r8 = Src0 >= C0
    rem = Src0 - r8 * C0
    r4 = rem >= C1
    return Spec(body=rem - r4 * C1)


def _spec_qlad_c():
    r2 = Src0 >= C1
    rem = Src0 - r2 * C1
    r1 = rem >= One
    rem2 = rem - r1
    return Spec(body=(C0 - rem2) * Src1)


def _spec_qdeq():
    ds = Src0 * C0
    sg = Src1 >= Zero
    sg3 = (sg + sg) - One
    return Spec(body=Src1 + sg3 * ds)


_DEFS = [
    ("ANT_QEXP_A", _spec_qexp_a, _ref_qexp_a),
    ("ANT_QEXP_B", _spec_qexp_b, _ref_qexp_b),
    ("ANT_QLAD_A", _spec_qlad_a, _ref_qlad_a),
    ("ANT_QLAD_B", _spec_qlad_b, _ref_qlad_b),
    ("ANT_QLAD_C", _spec_qlad_c, _ref_qlad_c),
    ("ANT_QDEQ", _spec_qdeq, _ref_qdeq),
]


def register_quant_ops():
    out = {}
    ver = dve_ver_for("TRN2")
    if [o.name for o in dve_ops_mod.OPS[:len(_DEFS)]] == [d[0] for d in _DEFS]:
        return {o.name: o for o in dve_ops_mod.OPS[:len(_DEFS)]}
    new_ops = []
    for name, mk_spec, ref in _DEFS:
        spec = mk_spec()
        compiled = lower(spec, ver=ver, perf_en=False, name=name)
        sha = compiled.sha(ver)
        op = DveOp(name, spec, subdim=False, uops_sha={ver: sha}, reference=ref)
        new_ops.append(op)
        dve_ops_mod.CUSTOM_DVE_SPECS[name] = spec
        dve_ops_mod.CUSTOM_DVE_REFS[name] = ref
        out[name] = op
    dve_ops_mod.OPS[:] = new_ops
    dve_ops_mod._SUB_OPCODE_FOR_NAME.clear()
    dve_ops_mod._SUB_OPCODE_FOR_NAME.update(
        {op.name: dve_ops_mod._CUSTOM_DVE_ROW_BASE + i for i, op in enumerate(new_ops)})
    return out




P = 128
NQ = 1024
NDC = 16
H = 16
NKC = 8
NTT = 2


def build(upto="all", dbg=()):
    ops = register_quant_ops()
    nc = bass.Bass("TRN2", target_bir_lowering=False)
    dbg = set(dbg)
    dbg_out = {}

    d_xT = nc.dram_tensor("xT", [2048, NQ], F32, kind="ExternalInput")
    d_xkT = nc.dram_tensor("xkT", [2048, NQ], F32, kind="ExternalInput")
    dW = {nm: nc.dram_tensor(nm, [2048, 2048], F32, kind="ExternalInput")
          for nm in ["WqT", "WkT", "WvT", "WoT", "Wf"]}
    d_cosq = nc.dram_tensor("cosq", [P, NQ], F32, kind="ExternalInput")
    d_sinq = nc.dram_tensor("sinq", [P, NQ], F32, kind="ExternalInput")
    d_cosk = nc.dram_tensor("cosk", [P, NQ], F32, kind="ExternalInput")
    d_sink = nc.dram_tensor("sink", [P, NQ], F32, kind="ExternalInput")
    d_mask = nc.dram_tensor("mask", [P, 2 * NKC], BF16, kind="ExternalInput")
    d_urow = nc.dram_tensor("urow", [1, H], F32, kind="ExternalInput")
    d_eye = nc.dram_tensor("eyef", [P, P], F32, kind="ExternalInput")
    d_out = nc.dram_tensor("outT", [2048, NQ], F32, kind="ExternalOutput")
    d_wscr = nc.dram_tensor("wscr", [2048, 2048], BF16, kind="Internal")

    from contextlib import ExitStack
    es = ExitStack()
    SV = Ctr(es.enter_context(nc.semaphore("sv")), "vector")
    SS = Ctr(es.enter_context(nc.semaphore("ss")), "scalar")
    SP = Ctr(es.enter_context(nc.semaphore("sp")), "tensor")
    SD = Ctr(es.enter_context(nc.semaphore("sd")), "sync", step=16)
    SG = Ctr(es.enter_context(nc.semaphore("sg")), "gpsimd", step=16)
    SL = Ctr(es.enter_context(nc.semaphore("sl")), "gpsimd")
    t_jk1 = es.enter_context(nc.sbuf_tensor("jk1", [P, 128], F32, side="left"))
    t_jv2 = es.enter_context(nc.sbuf_tensor("jv2", [P, 128], F32, side="left"))
    t_js2 = es.enter_context(nc.sbuf_tensor("js2", [P, 128], F32, side="left"))
    t_jg2 = es.enter_context(nc.sbuf_tensor("jg2", [P, 128], F32, side="left"))
    b_jk1 = Buf("jk1")
    b_jv2, b_js2, b_jg2 = Buf("jv2"), Buf("js2"), Buf("jg2")
    _settle = {
        "vector": lambda e: e.tensor_copy(t_jv2[:], t_jk1[:]),
        "scalar": lambda e: e.activation(t_js2[:], t_jk1[:], AF.Square),
        "gpsimd": lambda e: e.tensor_copy(t_jg2[0:64, :], t_jk1[0:64, :]),
    }
    pr = Prog(settle_fns=_settle,
              settle_bufs={"vector": (b_jk1, b_jv2), "scalar": (b_jk1, b_js2),
                           "gpsimd": (b_jk1, b_jg2)})

    def sbuf(name, shape, dt=F32, stack=es, side="right"):
        t = stack.enter_context(nc.sbuf_tensor(name, shape, dt, side=side))
        return t, Buf(name)

    def psum(name, shape, dt=F32, stack=es):
        t = stack.enter_context(nc.psum_tensor(name, shape, dt))
        return t, Buf(name)

    def tap(name, t, b, stack=es):
        if name not in dbg:
            return
        dh = nc.dram_tensor("dbg_" + name, list(t.shape), t.dtype, kind="ExternalOutput")
        dbg_out[name] = dh
        pr.step("sync", lambda e: e.dma_start(dh[:, :], t[:]), reads=[b], inc=SD,
                tag="tap_" + name)

    cN = {}
    cB = {}
    for nm, v in [("c2", 2.0), ("c448", 448.0), ("c4", 4.0), ("b35", 1e-35),
                  ("ceps", 1e-5), ("c1e6", 1e-6), ("cmh", -0.5)]:
        cN[nm], cB[nm] = sbuf(nm, [P, 1], side="left")
    for nm in ["ln2c", "iln2", "ccol", "sln", "bm9", "bp9", "rsqd", "r448",
               "r818", "r819"]:
        cN[nm], cB[nm] = sbuf(nm, [P, 1], side="left")
    t_eye, b_eye = sbuf("eye", [P, P], F32, side="left")
    t_ones1, b_ones1 = sbuf("ones1", [P, 1], BF16, side="left")
    t_onesr, b_onesr = sbuf("onesr", [1, P], F32, side="left")
    t_mask, b_mask = sbuf("maskT", [P, 2 * NKC], BF16, side="left")
    t_urow, b_urow = sbuf("urowT", [1, H], F32, side="left")

    def s0():
        def e_ms(e):
            for nm, v in [("c2", 2.0), ("c448", 448.0), ("c4", 4.0), ("b35", 1e-35),
                          ("ceps", 1e-5), ("c1e6", 1e-6), ("cmh", -0.5),
                          ("r818", 818.0), ("r819", 819.0)]:
                inst = e.memset(cN[nm][:], v)
            inst = e.memset(t_ones1[:], 1.0)
            inst = e.memset(t_onesr[:], 1.0)
            return e.memset(t_jk1[:], 1.0)
        pr.step("vector", e_ms, writes=[cB[n] for n in
                ["c2", "c448", "c4", "b35", "ceps", "c1e6", "cmh", "r818", "r819"]]
                + [b_ones1, b_onesr, b_jk1], inc=SV, tag="memsets")
        pr.step("scalar", lambda e: e.activation(cN["ln2c"][:], cN["c2"][:], AF.Ln),
                reads=[cB["c2"]], writes=[cB["ln2c"]], inc=SS, tag="ln2")
        pr.step("vector", lambda e: e.reciprocal(cN["iln2"][:], cN["ln2c"][:]),
                reads=[cB["ln2c"]], writes=[cB["iln2"]], inc=SV)
        def e_derived(e):
            e.tensor_scalar(cN["ccol"][:], cN["iln2"][:], 0.5, None, ALU.mult)
            e.tensor_scalar(cN["sln"][:], cN["ln2c"][:], -1.0, None, ALU.mult)
            e.tensor_scalar(cN["bm9"][:], cN["ln2c"][:], -9.0, None, ALU.mult)
            return e.tensor_scalar(cN["bp9"][:], cN["ln2c"][:], 9.0, None, ALU.mult)
        pr.step("vector", e_derived, reads=[cB["iln2"], cB["ln2c"]],
                writes=[cB["ccol"], cB["sln"], cB["bm9"], cB["bp9"]], inc=SV)
        pr.step("scalar", lambda e: e.activation(cN["rsqd"][:], cN["ln2c"][:], AF.Exp, scale=-3.5),
                reads=[cB["ln2c"]], writes=[cB["rsqd"]], inc=SS, tag="rsqd")
        def e_recips(e):
            e.reciprocal(cN["r448"][:], cN["c448"][:])
            e.reciprocal(cN["r818"][:], cN["r818"][:])
            return e.reciprocal(cN["r819"][:], cN["r819"][:])
        pr.step("vector", e_recips, reads=[cB["c448"], cB["r818"], cB["r819"]],
                writes=[cB["r448"], cB["r818"], cB["r819"]], inc=SV)
        def e_lddma(e):
            return [e.dma_start(t_eye[:], d_eye[:, :]),
                    e.dma_start(t_mask[:], d_mask[:, :]),
                    e.dma_start(t_urow[:], d_urow[:, :])]
        pr.step("sync", e_lddma, writes=[b_eye, b_mask, b_urow], inc=SD, ninc=3,
                tag="s0dma")

    def s1_set(tag, d_in, t_out, b_out, st):
        t_xf, b_xf = sbuf(tag + "xf", [P, NDC * NQ], F32, st)
        t_xb, b_xb = sbuf(tag + "xb", [P, NDC * NQ], BF16, st)
        t_sq, b_sq = [None, None], [None, None]
        for i in range(2):
            t_sq[i], b_sq[i] = sbuf(tag + f"sq{i}", [P, 512], BF16, st)
        t_r0, b_r0 = sbuf(tag + "r0", [1, NQ], F32, st)
        t_r1, b_r1 = sbuf(tag + "r1", [1, NQ], F32, st)
        t_r2, b_r2 = sbuf(tag + "r2", [1, NQ], F32, st)
        t_r3, b_r3 = sbuf(tag + "r3", [1, NQ], F32, st)
        t_A, b_A = sbuf(tag + "A", [1, NQ], F32, st)
        t_Br, b_Br = sbuf(tag + "B", [1, NQ], F32, st)
        t_s1, b_s1 = sbuf(tag + "s1", [P, 8 * 512], F32, st)
        ps_s, pb_s = psum(tag + "pss", [1, 512], F32, st)
        ps_q, pb_q = psum(tag + "psq", [1, 512], F32, st)
        ps_A, pb_A = psum(tag + "psA", [P, 512], F32, st)
        ps_B, pb_B = psum(tag + "psB", [P, 512], F32, st)

        def e_ld(e):
            return [e.dma_start(t_xf[:, dc * NQ:(dc + 1) * NQ],
                                d_in[dc * P:(dc + 1) * P, :]) for dc in range(NDC)]
        pr.step("sync", e_ld, writes=[b_xf], inc=SD, ninc=NDC, tag=tag + "ld")

        def e_cast(e):
            for dc in range(NDC):
                inst = e.tensor_copy(t_xb[:, dc * NQ:(dc + 1) * NQ],
                                     t_xf[:, dc * NQ:(dc + 1) * NQ])
            return inst
        pr.step("vector", e_cast, reads=[b_xf], writes=[b_xb], inc=SV, tag=tag + "cast")

        for tt in range(NTT):
            for dc in range(NDC):
                par = dc % 2
                pr.step("scalar", lambda e, dc=dc, tt=tt, par=par: e.activation(
                    t_sq[par][:], t_xb[:, dc * NQ + tt * 512: dc * NQ + tt * 512 + 512],
                    AF.Square),
                    reads=[b_xb], writes=[b_sq[par]], inc=SS, tag=f"{tag}sq{tt}_{dc}")
                def e_mm(e, dc=dc, tt=tt, par=par):
                    e.matmul(ps_s[:, :], t_ones1[:],
                             t_xb[:, dc * NQ + tt * 512: dc * NQ + tt * 512 + 512],
                             start=(dc == 0), stop=(dc == NDC - 1))
                    return e.matmul(ps_q[:, :], t_ones1[:], t_sq[par][:],
                                    start=(dc == 0), stop=(dc == NDC - 1))
                pr.step("tensor", e_mm, reads=[b_xb, b_sq[par], b_ones1],
                        writes=[pb_s, pb_q] if dc == NDC - 1 else [], inc=SP,
                        tag=f"{tag}mm{tt}_{dc}")
            sl = slice(tt * 512, tt * 512 + 512)
            def e_rows(e, sl=sl):
                e.tensor_scalar(t_r0[0:1, sl], ps_s[:, :], 1.0 / 2048.0, None, ALU.mult)
                return e.tensor_scalar(t_r1[0:1, sl], ps_q[:, :], 1.0 / 2048.0, None, ALU.mult)
            pr.step("vector", e_rows, reads=[pb_s, pb_q], writes=[b_r0, b_r1], inc=SV,
                    tag=tag + "rows", settle=True)
            pr.step("scalar", lambda e, sl=sl: e.activation(t_r2[0:1, sl], t_r0[0:1, sl],
                                                            AF.Square),
                    reads=[b_r0], writes=[b_r2], inc=SS, tag=tag + "m2", settle=True)
            pr.step("vector", lambda e, sl=sl: e.tensor_sub(t_r2[0:1, sl], t_r1[0:1, sl],
                                                            t_r2[0:1, sl]),
                    reads=[b_r1, b_r2], writes=[b_r2], inc=SV, tag=tag + "var", settle=True)
            pr.step("scalar", lambda e, sl=sl: e.activation(t_r3[0:1, sl], t_r2[0:1, sl],
                                                            AF.Ln, bias=cN["ceps"][0:1, :]),
                    reads=[b_r2, cB["ceps"]], writes=[b_r3], inc=SS, tag=tag + "lnv", settle=True)
            pr.step("scalar", lambda e, sl=sl: e.activation(t_A[0:1, sl], t_r3[0:1, sl],
                                                            AF.Exp, scale=-0.5),
                    reads=[b_r3], writes=[b_A], inc=SS, tag=tag + "rstd", settle=True)
            pr.step("vector", lambda e, sl=sl: e.tensor_mul(t_Br[0:1, sl], t_r0[0:1, sl],
                                                            t_A[0:1, sl]),
                    reads=[b_r0, b_A], writes=[b_Br], inc=SV, tag=tag + "Br", settle=True)
            pr.step("tensor", lambda e, sl=sl: e.matmul(ps_A[:, :], t_onesr[:], t_A[0:1, sl],
                                                        start=True, stop=True),
                    reads=[b_A, b_onesr], writes=[pb_A], inc=SP, tag=tag + "bA")
            pr.step("tensor", lambda e, sl=sl: e.matmul(ps_B[:, :], t_onesr[:], t_Br[0:1, sl],
                                                        start=True, stop=True),
                    reads=[b_Br, b_onesr], writes=[pb_B], inc=SP, tag=tag + "bB")
            for half in range(2):
                dcs = range(half * 8, half * 8 + 8)
                def e_mul(e, dcs=dcs, tt=tt):
                    for j, dc in enumerate(dcs):
                        inst = e.tensor_tensor(t_s1[:, j * 512:(j + 1) * 512],
                                               t_xf[:, dc * NQ + tt * 512: dc * NQ + tt * 512 + 512],
                                               ps_A[:, :], ALU.mult)
                    return inst
                pr.step("vector", e_mul, reads=[b_xf, pb_A], writes=[b_s1], inc=SV,
                        tag=f"{tag}ap1_{tt}_{half}")
                def e_sub(e, dcs=dcs, tt=tt):
                    for j, dc in enumerate(dcs):
                        inst = e.tensor_tensor(
                            t_out[:, dc * NQ + tt * 512: dc * NQ + tt * 512 + 512],
                            t_s1[:, j * 512:(j + 1) * 512], ps_B[:, :], ALU.subtract)
                    return inst
                pr.step("vector", e_sub, reads=[b_s1, pb_B], writes=[b_out], inc=SV,
                        tag=f"{tag}ap2_{tt}_{half}")

    def proj_weight_half(st, dname, half, t_wbf, b_wbf, t_wst, b_wst, tagp):
        for dc in range(NDC):
            par = dc % 2
            pr.step("sync", lambda e, dc=dc, half=half, par=par: e.dma_start(
                t_wst[par][:], dW[dname][dc * P:(dc + 1) * P,
                                         half * NQ:(half + 1) * NQ]),
                writes=[b_wst[par]], inc=SD, tag=f"{tagp}ld{half}_{dc}")
            pr.step("vector", lambda e, dc=dc, par=par: e.tensor_copy(
                t_wbf[:, dc * NQ:(dc + 1) * NQ], t_wst[par][:]),
                reads=[b_wst[par]], writes=[b_wbf], inc=SV, tag=f"{tagp}cast{half}_{dc}")

    def s2_qk(st, dname, tagp, d_cos, d_sin, ln_t, ln_b, out_t, out_b):
        t_wst, b_wst = [None] * 2, [None] * 2
        for i in range(2):
            t_wst[i], b_wst[i] = sbuf(f"{tagp}wst{i}", [P, NQ], F32, st)
        t_wbf, b_wbf = sbuf(tagp + "wbf", [P, NDC * NQ], BF16, st)
        t_qraw, b_qraw = [None] * 2, [None] * 2
        t_qswp, b_qswp = [None] * 2, [None] * 2
        t_t1, b_t1 = [None] * 2, [None] * 2
        t_t2, b_t2 = [None] * 2, [None] * 2
        for i in range(2):
            t_qraw[i], b_qraw[i] = sbuf(tagp + f"qraw{i}", [P, NQ], F32, st)
            t_qswp[i], b_qswp[i] = sbuf(tagp + f"qswp{i}", [P, NQ], F32, st)
            t_t1[i], b_t1[i] = sbuf(tagp + f"t1_{i}", [P, NQ], F32, st)
            t_t2[i], b_t2[i] = sbuf(tagp + f"t2_{i}", [P, NQ], F32, st)
        t_cq, b_cq = sbuf(tagp + "cosT", [P, NQ], F32, st)
        t_sn, b_sn = sbuf(tagp + "sinT", [P, NQ], F32, st)
        ps_p, pb_p = [None] * 2, [None] * 2
        for i in range(2):
            ps_p[i], pb_p[i] = psum(f"{tagp}ps{i}", [P, NQ], F32, st)

        def e_tab(e):
            return [e.dma_start(t_cq[:], d_cos[:, :]),
                    e.dma_start(t_sn[:], d_sin[:, :])]
        pr.step("sync", e_tab, writes=[b_cq, b_sn], inc=SD, ninc=2, tag=tagp + "tab")

        for half in range(2):
            proj_weight_half(st, dname, half, t_wbf, b_wbf, t_wst, b_wst, tagp)
            for jc in range(8):
                hh = half * 8 + jc
                par = hh % 2
                u = hh % 2
                def e_proj(e, jc=jc, par=par):
                    out = []
                    for tt in range(NTT):
                        for dc in range(NDC):
                            inst = e.matmul(
                                ps_p[par][:, tt * 512:(tt + 1) * 512],
                                t_wbf[:, dc * NQ + jc * P: dc * NQ + jc * P + P],
                                ln_t[:, dc * NQ + tt * 512: dc * NQ + tt * 512 + 512],
                                start=(dc == 0), stop=(dc == NDC - 1))
                        out.append(inst)
                    return out
                pr.step("tensor", e_proj, reads=[b_wbf, ln_b], writes=[pb_p[par]],
                        inc=SP, ninc=2, tag=f"{tagp}mm{hh}")
                pr.step("vector", lambda e, par=par, u=u: e.tensor_copy(
                    t_qraw[u][:], ps_p[par][:, :]),
                    reads=[pb_p[par]], writes=[b_qraw[u]], inc=SV, settle=True,
                    tag=f"{tagp}ev{hh}")
                def e_swp(e, u=u):
                    return [e.dma_start(t_qswp[u][0:64, :], t_qraw[u][64:128, :]),
                            e.dma_start(t_qswp[u][64:128, :], t_qraw[u][0:64, :])]
                pr.step("gpsimd", e_swp, reads=[b_qraw[u]], writes=[b_qswp[u]], inc=SG,
                        ninc=2, tag=f"{tagp}sw{hh}")
                pr.step("vector", lambda e, u=u: e.tensor_tensor(
                    t_t1[u][:], t_qraw[u][:], t_cq[:, :], ALU.mult),
                    reads=[b_qraw[u], b_cq], writes=[b_t1[u]], inc=SV,
                    tag=f"{tagp}c{hh}")
                pr.step("gpsimd", lambda e, u=u: e.tensor_tensor(
                    t_t2[u][:], t_qswp[u][:], t_sn[:, :], ALU.mult),
                    reads=[b_qswp[u], b_sn], writes=[b_t2[u]], inc=SL,
                    tag=f"{tagp}s{hh}")
                pr.step("vector", lambda e, hh=hh, u=u: e.tensor_tensor(
                    out_t[:, hh * NQ:(hh + 1) * NQ],
                    t_t1[u][:], t_t2[u][:], ALU.add),
                    reads=[b_t1[u], b_t2[u]], writes=[out_b], inc=SV,
                    tag=f"{tagp}a{hh}")

    def s2_v(st, ln_t, ln_b, v_t, v_b):
        t_wst, b_wst = [None] * 2, [None] * 2
        for i in range(2):
            t_wst[i], b_wst[i] = sbuf(f"vwst{i}", [P, NQ], F32, st)
        t_wbf, b_wbf = sbuf("vwbf", [P, NDC * NQ], BF16, st)
        ps_p, pb_p = [None] * 2, [None] * 2
        for i in range(2):
            ps_p[i], pb_p[i] = psum(f"vps{i}", [P, NQ], F32, st)
        for half in range(2):
            proj_weight_half(st, "WvT", half, t_wbf, b_wbf, t_wst, b_wst, "Wv")
            for kc in range(NKC):
                par = kc % 2
                def e_vp(e, kc=kc, par=par):
                    out = []
                    for jt in range(2):
                        for dc in range(NDC):
                            inst = e.matmul(
                                ps_p[par][:, jt * 512:(jt + 1) * 512],
                                ln_t[:, dc * NQ + kc * P: dc * NQ + kc * P + P],
                                t_wbf[:, dc * NQ + jt * 512: dc * NQ + jt * 512 + 512],
                                start=(dc == 0), stop=(dc == NDC - 1))
                        out.append(inst)
                    return out
                pr.step("tensor", e_vp, reads=[b_wbf, ln_b], writes=[pb_p[par]],
                        inc=SP, ninc=2, tag=f"Vmm{half}_{kc}")
                jf0 = half * NQ
                pr.step("vector", lambda e, kc=kc, jf0=jf0, par=par: e.tensor_copy(
                    v_t[:, kc * 2048 + jf0: kc * 2048 + jf0 + NQ], ps_p[par][:, :]),
                    reads=[pb_p[par]], writes=[v_b], inc=SV,
                    tag=f"Vev{half}_{kc}")

    def s3(st, t_QT, b_QT, t_KT, b_KT, t_V, b_V, t_AT, b_AT):
        t_ES, b_ES = [None] * 2, [None] * 2
        for i in range(2):
            t_ES[i], b_ES[i] = sbuf(f"ExpS{i}", [P, NKC * NQ], BF16, st)
        t_rs, b_rs = sbuf("rssb", [2, NQ], F32, st)
        t_rr, b_rr = sbuf("rrrow", [1, NQ], F32, st)
        t_rrb, b_rrb = sbuf("rrb", [P, NQ], F32, st)
        t_vrow, b_vrow = sbuf("vrow", [1, 2048], BF16, st)
        t_sr1, b_sr1 = sbuf("sr1", [1, 256], F32, st, side="left")
        t_srB8, b_srB8 = sbuf("srB8", [1, H * 256], BF16, st, side="left")
        t_veA, b_veA = sbuf("veA", [1, H * P], BF16, st, side="left")
        t_p0, b_p0 = sbuf("pcr0", [1, H], F32, st)
        t_p1, b_p1 = sbuf("pcr1", [1, H], F32, st)
        t_p2, b_p2 = sbuf("pcr2", [1, H], F32, st)
        t_p3, b_p3 = sbuf("pcr3", [1, H], F32, st)
        t_ev, b_ev = sbuf("evrow", [1, H], F32, st)
        t_evb, b_evb = sbuf("evrowB", [1, H], BF16, st)
        ps_Sb, _pbSb = psum("psSb", [P, 2048], F32, st)
        pb_S = [Buf(f"psSb{i}") for i in range(4)]
        ps_o, pb_o = [None] * 2, [None] * 2
        for i in range(2):
            ps_o[i], pb_o[i] = psum(f"psO{i}", [P, 512], F32, st)
        ps_rs, pb_rs = [None] * 2, [None] * 2
        for i in range(2):
            ps_rs[i], pb_rs[i] = psum(f"psRS{i}", [2, 512], F32, st)

        pr.step("gpsimd", lambda e: e.dma_start(t_vrow[0:1, :],
                                                t_V[76:77, 1 * 2048:2 * 2048]),
                reads=[b_V], writes=[b_vrow], inc=SG, tag="vrow")

        for h in range(H):
            he = h % 2
            for kc in range(NKC):
                base = (kc % 2) * 1024
                a = (kc % 2) * 2
                def e_sc(e, h=h, kc=kc, base=base):
                    return [e.matmul(
                        ps_Sb[:, base + tt * 512: base + tt * 512 + 512],
                        t_KT[:, h * NQ + kc * P: h * NQ + kc * P + P],
                        t_QT[:, h * NQ + tt * 512: h * NQ + tt * 512 + 512],
                        start=True, stop=True) for tt in range(NTT)]
                pr.step("tensor", e_sc, reads=[b_KT, b_QT],
                        writes=[pb_S[a], pb_S[a + 1]], inc=SP, ninc=2,
                        tag=f"sc{h}_{kc}")
                pr.step("scalar", lambda e, h=h, kc=kc, base=base, he=he: e.activation(
                    t_ES[he][:, kc * NQ: kc * NQ + 1024],
                    ps_Sb[:, base: base + 1024], AF.Exp, scale=cN["rsqd"][:, :]),
                    reads=[pb_S[a], pb_S[a + 1], cB["rsqd"]], writes=[b_ES[he]], inc=SS,
                    tag=f"ex{h}_{kc}")
            def e_rs(e, he=he):
                out = []
                for tt in range(NTT):
                    for kc in range(NKC):
                        inst = e.matmul(ps_rs[tt][:, :], t_mask[:, 2 * kc:2 * kc + 2],
                                        t_ES[he][:, kc * NQ + tt * 512: kc * NQ + tt * 512 + 512],
                                        start=(kc == 0), stop=(kc == NKC - 1))
                    out.append(inst)
                return out
            pr.step("tensor", e_rs, reads=[b_ES[he], b_mask],
                    writes=[pb_rs[0], pb_rs[1]], inc=SP, ninc=2, tag=f"rs{h}")
            def e_rsev(e):
                e.tensor_copy(t_rs[:, 0:512], ps_rs[0][:, :])
                return e.tensor_copy(t_rs[:, 512:1024], ps_rs[1][:, :])
            pr.step("vector", e_rsev, reads=[pb_rs[0], pb_rs[1]], writes=[b_rs], inc=SV,
                    tag=f"rsev{h}", settle=True)
            def e_stash(e, h=h, he=he):
                return [e.dma_start(t_p0[0:1, h:h + 1], t_rs[0:1, 1023:1024]),
                        e.dma_start(t_p1[0:1, h:h + 1], t_rs[1:2, 1023:1024]),
                        e.dma_start(t_sr1[0:1, :], t_rs[1:2, 768:1024]),
                        e.dma_start(t_evb[0:1, h:h + 1],
                                    t_ES[he][76:77, 1 * NQ + 1023: 1 * NQ + 1024])]
            pr.step("gpsimd", e_stash, reads=[b_rs, b_ES[he]],
                    writes=[b_p0, b_p1, b_sr1, b_evb], inc=SG, ninc=4, tag=f"stash{h}")
            def e_pv(e, h=h, he=he):
                out = []
                for tt in range(NTT):
                    for kc in range(NKC):
                        inst = e.matmul(ps_o[tt][:, :],
                                        t_V[:, kc * 2048 + h * P: kc * 2048 + h * P + P],
                                        t_ES[he][:, kc * NQ + tt * 512: kc * NQ + tt * 512 + 512],
                                        start=(kc == 0), stop=(kc == NKC - 1))
                    out.append(inst)
                return out
            pr.step("tensor", e_pv, reads=[b_ES[he], b_V], writes=[pb_o[0], pb_o[1]],
                    inc=SP, ninc=2, tag=f"pv{h}")
            pr.step("vector", lambda e: e.reciprocal(t_rr[0:1, :], t_rs[0:1, :]),
                    reads=[b_rs], writes=[b_rr], inc=SV, tag=f"rr{h}", settle=True)
            def e_rrb(e):
                return [e.matmul(ps_Sb[:, tt * 512:(tt + 1) * 512], t_onesr[:],
                                 t_rr[0:1, tt * 512:(tt + 1) * 512],
                                 start=True, stop=True) for tt in range(NTT)]
            pr.step("tensor", e_rrb, reads=[b_rr, b_onesr], writes=[pb_S[0], pb_S[1]],
                    inc=SP, ninc=2, tag=f"rrb{h}")
            def e_rrev(e, h=h):
                for tt in range(NTT):
                    e.tensor_copy(t_rrb[:, tt * 512:(tt + 1) * 512],
                                  ps_Sb[:, tt * 512:(tt + 1) * 512])
                return e.tensor_tensor(t_srB8[0:1, h * 256:(h + 1) * 256],
                                       t_sr1[0:1, :], t_rr[0:1, 768:1024], ALU.mult)
            pr.step("vector", e_rrev, reads=[pb_S[0], pb_S[1], b_sr1, b_rr],
                    writes=[b_rrb, b_srB8], inc=SV, tag=f"rrev{h}")
            def e_oev(e, h=h):
                return [e.tensor_tensor(
                    t_AT[:, h * NQ + tt * 512: h * NQ + tt * 512 + 512],
                    ps_o[tt][:, :], t_rrb[:, tt * 512:(tt + 1) * 512], ALU.mult)
                    for tt in range(NTT)]
            pr.step("vector", e_oev, reads=[pb_o[0], pb_o[1], b_rrb], writes=[b_AT],
                    inc=SV, ninc=2, tag=f"oev{h}")

        pr.step("vector", lambda e: e.tensor_copy(t_ev[0:1, :], t_evb[0:1, :]),
                reads=[b_evb], writes=[b_ev], inc=SV, tag="evcD", settle=True)
        def e_den(e):
            e.tensor_scalar(t_p2[0:1, :], t_p1[0:1, :], cN["r818"][0:1, 0:1], None,
                            ALU.mult)
            return e.tensor_scalar(t_p3[0:1, :], t_p0[0:1, :], cN["c1e6"][0:1, 0:1],
                                   None, ALU.mult)
        pr.step("vector", e_den, reads=[b_p0, b_p1, cB["r818"], cB["c1e6"]],
                writes=[b_p2, b_p3], inc=SV, tag="denD", settle=True)
        pr.step("vector", lambda e: e.tensor_tensor(t_p2[0:1, :], t_p2[0:1, :],
                                                    t_p3[0:1, :], ALU.add),
                reads=[b_p2, b_p3], writes=[b_p2], inc=SV, tag="den2D", settle=True)
        pr.step("vector", lambda e: e.reciprocal(t_p3[0:1, :], t_p2[0:1, :]),
                reads=[b_p2], writes=[b_p3], inc=SV, tag="rdenD", settle=True)
        pr.step("vector", lambda e: e.tensor_tensor(t_p2[0:1, :], t_ev[0:1, :],
                                                    t_p3[0:1, :], ALU.mult),
                reads=[b_p3, b_ev], writes=[b_p2], inc=SV, tag="pmD", settle=True)
        pr.step("vector", lambda e: e.tensor_scalar(t_p2[0:1, :], t_p2[0:1, :],
                                                    1.0, 0.0, ALU.min, ALU.max),
                reads=[b_p2], writes=[b_p2], inc=SV, tag="pD", settle=True)
        pr.step("vector", lambda e: e.tensor_tensor(t_p3[0:1, :], t_urow[0:1, :],
                                                    t_p2[0:1, :], ALU.is_lt),
                reads=[b_p2, b_urow], writes=[b_p3], inc=SV, tag="bcD", settle=True)
        pr.step("vector", lambda e: e.tensor_scalar(t_p3[0:1, :], t_p3[0:1, :],
                                                    cN["r819"][0:1, 0:1], None, ALU.mult),
                reads=[b_p3, cB["r819"]], writes=[b_p3], inc=SV, tag="bernD", settle=True)
        def e_veA(e):
            for h in range(H):
                inst = e.tensor_scalar(t_veA[0:1, h * P:(h + 1) * P],
                                       t_vrow[0:1, h * P:(h + 1) * P],
                                       t_p3[0:1, h:h + 1], None, ALU.mult)
            return inst
        pr.step("vector", e_veA, reads=[b_vrow, b_p3], writes=[b_veA], inc=SV,
                tag="veAD", settle=True)
        for rd in range(2):
            def e_cm(e, rd=rd):
                return [e.matmul(ps_Sb[:, i * 256:(i + 1) * 256],
                                 t_veA[0:1, (rd * 8 + i) * P:(rd * 8 + i + 1) * P],
                                 t_srB8[0:1, (rd * 8 + i) * 256:(rd * 8 + i + 1) * 256],
                                 start=True, stop=True) for i in range(8)]
            pr.step("tensor", e_cm, reads=[b_veA, b_srB8],
                    writes=[pb_S[0], pb_S[1], pb_S[2], pb_S[3]], inc=SP, ninc=8,
                    tag=f"corrM{rd}")
            def e_cu(e, rd=rd):
                for i in range(8):
                    h = rd * 8 + i
                    inst = e.tensor_tensor(
                        t_AT[:, h * NQ + 768: h * NQ + 1024],
                        t_AT[:, h * NQ + 768: h * NQ + 1024],
                        ps_Sb[:, i * 256:(i + 1) * 256], ALU.add)
                return inst
            pr.step("vector", e_cu, reads=[pb_S[0], pb_S[1], pb_S[2], pb_S[3], b_AT],
                    writes=[b_AT], inc=SV, tag=f"corrU{rd}")

    def s4_proj(st, t_AT, b_AT, t_abf, b_abf):
        t_wst, b_wst = [None] * 2, [None] * 2
        for i in range(2):
            t_wst[i], b_wst[i] = sbuf(f"o_wst{i}", [P, NQ], F32, st)
        t_wbf, b_wbf = sbuf("o_wbf", [P, NDC * NQ], BF16, st)
        ps_p, pb_p = [None] * 2, [None] * 2
        for i in range(2):
            ps_p[i], pb_p[i] = psum(f"o_ps{i}", [P, 512], F32, st)

        for half in range(2):
            for dc in range(NDC):
                par = dc % 2
                pr.step("sync", lambda e, dc=dc, half=half, par=par: e.dma_start(
                    t_wst[par][:], dW["WoT"][dc * P:(dc + 1) * P, half * NQ:(half + 1) * NQ]),
                    writes=[b_wst[par]], inc=SD, tag=f"Wold{half}_{dc}")
                pr.step("vector", lambda e, dc=dc, par=par: e.tensor_copy(
                    t_wbf[:, dc * NQ:(dc + 1) * NQ], t_wst[par][:]),
                    reads=[b_wst[par]], writes=[b_wbf], inc=SV, tag=f"Wocast{half}_{dc}")
            for jc in range(8):
                oc = half * 8 + jc
                for tt in range(NTT):
                    par = (oc * NTT + tt) % 2
                    def e_mm(e, jc=jc, tt=tt, par=par):
                        for dc in range(NDC):
                            inst = e.matmul(
                                ps_p[par][:, :],
                                t_wbf[:, dc * NQ + jc * P: dc * NQ + jc * P + P],
                                t_AT[:, dc * NQ + tt * 512: dc * NQ + tt * 512 + 512],
                                start=(dc == 0), stop=(dc == NDC - 1))
                        return inst
                    pr.step("tensor", e_mm, reads=[b_wbf, b_AT], writes=[pb_p[par]],
                            inc=SP, tag=f"Womm{oc}_{tt}")
                    pr.step("vector", lambda e, oc=oc, tt=tt, par=par: e.tensor_copy(
                        t_abf[:, oc * NQ + tt * 512: oc * NQ + tt * 512 + 512],
                        ps_p[par][:, :]),
                        reads=[pb_p[par]], writes=[b_abf], inc=SV, tag=f"Woev{oc}_{tt}")

    def ln_res(st, tag, t_in, b_in, res_dram, res_sbuf, t_dst, b_dst):
        t_sq, b_sq = [None] * 2, [None] * 2
        for i in range(2):
            t_sq[i], b_sq[i] = sbuf(tag + f"sq{i}", [P, 512], BF16, st)
        t_r0, b_r0 = sbuf(tag + "r0", [1, NQ], F32, st)
        t_r1, b_r1 = sbuf(tag + "r1", [1, NQ], F32, st)
        t_r2, b_r2 = sbuf(tag + "r2", [1, NQ], F32, st)
        t_r3, b_r3 = sbuf(tag + "r3", [1, NQ], F32, st)
        t_A, b_A = sbuf(tag + "A", [1, NQ], F32, st)
        t_Br, b_Br = sbuf(tag + "B", [1, NQ], F32, st)
        t_s1, b_s1 = [None] * 2, [None] * 2
        for i in range(2):
            t_s1[i], b_s1[i] = sbuf(tag + f"s1_{i}", [P, 4 * 512], F32, st)
        if res_dram is not None:
            t_xst, b_xst = [None] * 2, [None] * 2
            for i in range(2):
                t_xst[i], b_xst[i] = sbuf(tag + f"xst{i}", [P, 4 * 512], F32, st)
        ps_s, pb_s = psum(tag + "pss", [1, 512], F32, st)
        ps_q, pb_q = psum(tag + "psq", [1, 512], F32, st)
        ps_A, pb_A = psum(tag + "psA", [P, 512], F32, st)
        ps_B, pb_B = psum(tag + "psB", [P, 512], F32, st)
        for tt in range(NTT):
            for dc in range(NDC):
                par = dc % 2
                pr.step("scalar", lambda e, dc=dc, tt=tt, par=par: e.activation(
                    t_sq[par][:], t_in[:, dc * NQ + tt * 512: dc * NQ + tt * 512 + 512],
                    AF.Square),
                    reads=[b_in], writes=[b_sq[par]], inc=SS, tag=f"{tag}sq{tt}_{dc}")
                def e_mm(e, dc=dc, tt=tt, par=par):
                    e.matmul(ps_s[:, :], t_ones1[:],
                             t_in[:, dc * NQ + tt * 512: dc * NQ + tt * 512 + 512],
                             start=(dc == 0), stop=(dc == NDC - 1))
                    return e.matmul(ps_q[:, :], t_ones1[:], t_sq[par][:],
                                    start=(dc == 0), stop=(dc == NDC - 1))
                pr.step("tensor", e_mm, reads=[b_in, b_sq[par], b_ones1],
                        writes=([pb_s, pb_q] if dc == NDC - 1 else []), inc=SP,
                        tag=f"{tag}mm{tt}_{dc}")
            sl = slice(tt * 512, tt * 512 + 512)
            def e_rows(e, sl=sl):
                e.tensor_scalar(t_r0[0:1, sl], ps_s[:, :], 1.0 / 2048.0, None, ALU.mult)
                return e.tensor_scalar(t_r1[0:1, sl], ps_q[:, :], 1.0 / 2048.0, None, ALU.mult)
            pr.step("vector", e_rows, reads=[pb_s, pb_q], writes=[b_r0, b_r1], inc=SV,
                    tag=tag + "rows", settle=True)
            pr.step("scalar", lambda e, sl=sl: e.activation(t_r2[0:1, sl], t_r0[0:1, sl],
                                                            AF.Square),
                    reads=[b_r0], writes=[b_r2], inc=SS, tag=tag + "m2", settle=True)
            pr.step("vector", lambda e, sl=sl: e.tensor_sub(t_r2[0:1, sl], t_r1[0:1, sl],
                                                            t_r2[0:1, sl]),
                    reads=[b_r1, b_r2], writes=[b_r2], inc=SV, tag=tag + "var", settle=True)
            pr.step("scalar", lambda e, sl=sl: e.activation(t_r3[0:1, sl], t_r2[0:1, sl],
                                                            AF.Ln, bias=cN["ceps"][0:1, :]),
                    reads=[b_r2, cB["ceps"]], writes=[b_r3], inc=SS, tag=tag + "lnv", settle=True)
            pr.step("scalar", lambda e, sl=sl: e.activation(t_A[0:1, sl], t_r3[0:1, sl],
                                                            AF.Exp, scale=-0.5),
                    reads=[b_r3], writes=[b_A], inc=SS, tag=tag + "rstd", settle=True)
            pr.step("vector", lambda e, sl=sl: e.tensor_mul(t_Br[0:1, sl], t_r0[0:1, sl],
                                                            t_A[0:1, sl]),
                    reads=[b_r0, b_A], writes=[b_Br], inc=SV, tag=tag + "Br", settle=True)
            pr.step("tensor", lambda e, sl=sl: e.matmul(ps_A[:, :], t_onesr[:], t_A[0:1, sl],
                                                        start=True, stop=True),
                    reads=[b_A, b_onesr], writes=[pb_A], inc=SP, tag=tag + "bA")
            pr.step("tensor", lambda e, sl=sl: e.matmul(ps_B[:, :], t_onesr[:], t_Br[0:1, sl],
                                                        start=True, stop=True),
                    reads=[b_Br, b_onesr], writes=[pb_B], inc=SP, tag=tag + "bB")
            for half in range(4):
                dcs = list(range(half * 4, half * 4 + 4))
                if res_dram is not None:
                    def e_ld(e, dcs=dcs, tt=tt, half=half):
                        return [e.dma_start(
                            t_xst[half % 2][:, j * 512:(j + 1) * 512],
                            res_dram[dc * P:(dc + 1) * P, tt * 512: tt * 512 + 512])
                            for j, dc in enumerate(dcs)]
                    pr.step("sync", e_ld, writes=[b_xst[half % 2]], inc=SD, ninc=4,
                            tag=f"{tag}ld{tt}_{half}")
                hp = half % 2
                def e_mul(e, dcs=dcs, tt=tt, hp=hp):
                    for j, dc in enumerate(dcs):
                        inst = e.tensor_tensor(
                            t_s1[hp][:, j * 512:(j + 1) * 512],
                            t_in[:, dc * NQ + tt * 512: dc * NQ + tt * 512 + 512],
                            ps_A[:, :], ALU.mult)
                    return inst
                pr.step("vector", e_mul, reads=[b_in, pb_A], writes=[b_s1[hp]], inc=SV,
                        tag=f"{tag}ap1_{tt}_{half}")
                def e_sub(e, dcs=dcs, tt=tt, hp=hp):
                    for j, dc in enumerate(dcs):
                        inst = e.tensor_tensor(t_s1[hp][:, j * 512:(j + 1) * 512],
                                               t_s1[hp][:, j * 512:(j + 1) * 512],
                                               ps_B[:, :], ALU.subtract)
                    return inst
                pr.step("vector", e_sub, reads=[b_s1[hp], pb_B], writes=[b_s1[hp]], inc=SV,
                        tag=f"{tag}ap2_{tt}_{half}")
                eng3, ctr3 = (("gpsimd", SL) if half % 2 else ("vector", SV))
                if res_dram is not None:
                    def e_add(e, dcs=dcs, tt=tt, half=half, hp=hp):
                        for j, dc in enumerate(dcs):
                            inst = e.tensor_tensor(
                                t_dst[:, dc * NQ + tt * 512: dc * NQ + tt * 512 + 512],
                                t_s1[hp][:, j * 512:(j + 1) * 512],
                                t_xst[half % 2][:, j * 512:(j + 1) * 512], ALU.add)
                        return inst
                    pr.step(eng3, e_add, reads=[b_s1[hp], b_xst[half % 2]], writes=[b_dst],
                            inc=ctr3, tag=f"{tag}ap3_{tt}_{half}")
                else:
                    t_res, b_res = res_sbuf
                    def e_add(e, dcs=dcs, tt=tt, hp=hp):
                        for j, dc in enumerate(dcs):
                            inst = e.tensor_tensor(
                                t_dst[:, dc * NQ + tt * 512: dc * NQ + tt * 512 + 512],
                                t_s1[hp][:, j * 512:(j + 1) * 512],
                                t_res[:, dc * NQ + tt * 512: dc * NQ + tt * 512 + 512],
                                ALU.add)
                        return inst
                    rd = [b_s1[hp]] + ([b_res] if b_res is not b_dst else [])
                    pr.step(eng3, e_add, reads=rd, writes=[b_dst],
                            inc=ctr3, tag=f"{tag}ap3_{tt}")

    d_eyeb = nc.dram_tensor("eyeb", [P, P], BF16, kind="ExternalInput")

    def s5(st, t_x1, b_x1, t_xdeqT, b_xdeqT):
        t_eyB, b_eyB = sbuf("eyeBs", [P, P], BF16, st)
        pr.step("sync", lambda e: e.dma_start(t_eyB[:], d_eyeb[:, :]), writes=[b_eyB],
                inc=SD, tag="eyebld")
        t_xtok, b_xtok = [None] * 2, [None] * 2
        t_xdb, b_xdb = [None] * 2, [None] * 2
        t_st, b_st = [None] * 2, [None] * 2
        t_am, b_am = [None] * 2, [None] * 2
        t_rc, b_rc = [None] * 2, [None] * 2
        t_s0, b_s0 = [None] * 2, [None] * 2
        for i in range(2):
            t_xtok[i], b_xtok[i] = sbuf(f"xtok{i}", [P, 2048], F32, st)
            t_xdb[i], b_xdb[i] = sbuf(f"xdtok{i}", [P, 2048], BF16, st)
            t_st[i], b_st[i] = sbuf(f"tstat{i}", [P, 4], F32, st)
            t_am[i], b_am[i] = sbuf(f"tamax{i}", [P, 16], F32, st)
            t_rc[i], b_rc[i] = sbuf(f"trc{i}", [P, 16], F32, st)
            t_s0[i], b_s0[i] = sbuf(f"ts0{i}", [P, 16], F32, st)
        ps_tr, pb_tr = [None] * 2, [None] * 2
        ps_tb, pb_tb = [None] * 2, [None] * 2
        for i in range(2):
            ps_tr[i], pb_tr[i] = psum(f"pstr{i}", [P, 512], F32, st)
            ps_tb[i], pb_tb[i] = psum(f"pstb{i}", [P, 512], BF16, st)
        t_v, b_v0, b_v1 = _q["v"]
        t_sp, b_sp = _q["sp"]
        t_rp, b_rp = _q["rp"]
        for tc in range(8):
            u = tc % 2
            for g in range(4):
                gp = g % 2
                def e_tr(e, g=g, tc=tc, gp=gp):
                    return [e.transpose(
                        ps_tr[gp][:, j * P:(j + 1) * P],
                        t_x1[:, (4 * g + j) * NQ + tc * P: (4 * g + j) * NQ + tc * P + P],
                        t_eye[:]) for j in range(4)]
                pr.step("tensor", e_tr, reads=[b_x1, b_eye], writes=[pb_tr[gp]], inc=SP,
                        ninc=4, tag=f"tr{tc}_{g}")
                pr.step("vector", lambda e, g=g, gp=gp, u=u: e.tensor_copy(
                    t_xtok[u][:, g * 512:(g + 1) * 512], ps_tr[gp][:, :]),
                    reads=[pb_tr[gp]], writes=[b_xtok[u]], inc=SV, tag=f"trev{tc}_{g}")
            pr.step("scalar", lambda e, u=u: e.activation(
                t_v[:], t_xtok[u][:], AF.Identity, accum_out=t_st[u][:, 0:1]),
                reads=[b_xtok[u]], writes=[b_v0, b_v1, b_st[u]], inc=SS,
                tag=f"tsum{tc}", settle=True)
            pr.step("vector", lambda e, u=u: e.tensor_scalar(
                t_st[u][:, 3:4], t_st[u][:, 0:1], -1.0 / 2048.0, None, ALU.mult),
                reads=[b_st[u]], writes=[b_st[u]], inc=SV, tag=f"tnm{tc}", settle=True)
            pr.step("scalar", lambda e, u=u: e.activation(
                t_xtok[u][:], t_xtok[u][:], AF.Identity, bias=t_st[u][:, 3:4]),
                reads=[b_st[u]], writes=[b_xtok[u]], inc=SS, tag=f"tshift{tc}")
            pr.step("scalar", lambda e, u=u: e.activation(
                t_sp[:], t_xtok[u][:], AF.Square, accum_out=t_st[u][:, 1:2]),
                reads=[b_xtok[u]], writes=[b_sp, b_st[u]], inc=SS,
                tag=f"tsq{tc}", settle=True)
            pr.step("scalar", lambda e, u=u: e.activation(
                t_st[u][:, 2:3], t_st[u][:, 1:2], AF.Ln, bias=cN["ceps"][:, 0:1],
                scale=1.0 / 2048.0),
                reads=[b_st[u], cB["ceps"]], writes=[b_st[u]], inc=SS,
                tag=f"tlv{tc}", settle=True)
            pr.step("scalar", lambda e, u=u: e.activation(
                t_st[u][:, 2:3], t_st[u][:, 2:3], AF.Exp, scale=-0.5),
                reads=[b_st[u]], writes=[b_st[u]], inc=SS, tag=f"trs{tc}", settle=True)
            pr.step("vector", lambda e, u=u: e.tensor_reduce(
                t_am[u][:, :], t_xtok[u][:].rearrange("p (c f) -> p c f", f=P), AX.X,
                ALU.max, apply_absolute_value=True),
                reads=[b_xtok[u]], writes=[b_am[u]], inc=SV, tag=f"tam{tc}", settle=True)
            pr.step("vector", lambda e, u=u: e.reciprocal(t_rc[u][:, :], t_am[u][:, :]),
                    reads=[b_am[u]], writes=[b_rc[u]], inc=SV, tag=f"trc{tc}", settle=True)
            def e_rs(e, u=u):
                e.tensor_scalar(t_rc[u][:, :], t_rc[u][:, :], 448.0, None, ALU.mult)
                return e.tensor_scalar(t_s0[u][:, :], t_am[u][:, :],
                                       cN["r448"][:, 0:1], None, ALU.mult)
            pr.step("vector", e_rs, reads=[b_rc[u], b_am[u], cB["r448"]],
                    writes=[b_rc[u], b_s0[u]], inc=SV, tag=f"trs0{tc}", settle=True)
            qtile(t_xtok[u], [b_xtok[u]], t_s0[u], b_s0[u], t_rc[u], b_rc[u], tg=f"x{tc}")
            pr.step("scalar", lambda e, u=u: e.activation(
                t_xdb[u][:], t_rp[:], AF.Identity, scale=t_st[u][:, 2:3]),
                reads=[b_rp, b_st[u]], writes=[b_xdb[u]], inc=SS, tag=f"txc{tc}")
            for g in range(4):
                gp = g % 2
                def e_tb(e, g=g, gp=gp, u=u):
                    return [e.transpose(
                        ps_tb[gp][:, j * P:(j + 1) * P],
                        t_xdb[u][:, (4 * g + j) * P:(4 * g + j + 1) * P], t_eyB[:])
                        for j in range(4)]
                pr.step("tensor", e_tb, reads=[b_xdb[u], b_eyB], writes=[pb_tb[gp]],
                        inc=SP, ninc=4, tag=f"trb{tc}_{g}")
                def e_tbe(e, g=g, gp=gp, tc=tc):
                    return [e.tensor_copy(
                        t_xdeqT[:, (4 * g + j) * NQ + tc * P: (4 * g + j) * NQ + tc * P + P],
                        ps_tb[gp][:, j * P:(j + 1) * P]) for j in range(4)]
                pr.step("vector", e_tbe, reads=[pb_tb[gp]], writes=[b_xdeqT], inc=SV,
                        ninc=4, tag=f"trbe{tc}_{g}")

    _q = {}
    _q_stack = [es]

    def qtile(t_x, bx_list, t_s0col, b_s0col, t_rc, b_rc, tg=""):
        t_v, b_v0, b_v1 = _q["v"]
        t_l, b_l = _q["l"]
        t_sp, b_sp = _q["sp"]
        t_rp, b_rp = _q["rp"]
        t_t1, b_t1 = _q["t1"]
        t_t2, b_t2 = _q["t2"]
        def e_v_sc(e):
            for c in range(8):
                inst = e.activation(t_v[:, c * P:(c + 1) * P], t_x[:, c * P:(c + 1) * P],
                                    AF.Identity, scale=t_rc[:, c:c + 1])
            return inst
        pr.step("scalar", e_v_sc, reads=bx_list + [b_rc], writes=[b_v0], inc=SS,
                tag=tg + "qvs")
        def e_v_dv(e):
            for c in range(8, 16):
                inst = e.tensor_scalar(t_v[:, c * P:(c + 1) * P], t_x[:, c * P:(c + 1) * P],
                                       t_rc[:, c:c + 1], None, ALU.mult)
            return inst
        pr.step("vector", e_v_dv, reads=bx_list + [b_rc], writes=[b_v1], inc=SV,
                tag=tg + "qvd")
        pr.step("scalar", lambda e: e.activation(t_l[:], t_v[:], AF.Square),
                reads=[b_v0, b_v1], writes=[b_l], inc=SS, tag=tg + "qsq")
        pr.step("scalar", lambda e: e.activation(t_l[:], t_l[:], AF.Ln, bias=cN["b35"][:, 0:1]),
                reads=[b_l, cB["b35"]], writes=[b_l], inc=SS, tag=tg + "qln")
        pr.step("vector", lambda e: e._custom_dve(ops["ANT_QEXP_A"], out=t_t1[:], in0=t_l[:],
                                                  in1=cN["c4"][:, 0:1], s0=cN["ccol"][:, 0:1],
                                                  s1=6.0, imm2=8.0),
                reads=[b_l, cB["c4"], cB["ccol"]], writes=[b_t1], inc=SV, tag=tg + "qa")
        pr.step("vector", lambda e: e._custom_dve(ops["ANT_QEXP_B"], out=t_t2[:], in0=t_t1[:],
                                                  in1=t_l[:], s0=cN["ccol"][:, 0:1],
                                                  s1=6.0, imm2=2.0),
                reads=[b_t1, b_l, cB["ccol"]], writes=[b_t2], inc=SV, tag=tg + "qb")
        pr.step("scalar", lambda e: e.activation(t_sp[:], t_t2[:], AF.Exp,
                                                 bias=cN["bm9"][:, :], scale=cN["ln2c"][:, :]),
                reads=[b_t2, cB["bm9"], cB["ln2c"]], writes=[b_sp], inc=SS, tag=tg + "qst")
        pr.step("scalar", lambda e: e.activation(t_rp[:], t_t2[:], AF.Exp,
                                                 bias=cN["bp9"][:, :], scale=cN["sln"][:, :]),
                reads=[b_t2, cB["bp9"], cB["sln"]], writes=[b_rp], inc=SS, tag=tg + "qrt")
        pr.step("vector", lambda e: e._custom_dve(ops["ANT_QLAD_A"], out=t_t1[:], in0=t_v[:],
                                                  in1=t_rp[:], s0=0.5, s1=0.0, imm2=16.0),
                reads=[b_v0, b_v1, b_rp], writes=[b_t1], inc=SV, tag=tg + "qla")
        pr.step("vector", lambda e: e._custom_dve(ops["ANT_QLAD_B"], out=t_t2[:], in0=t_t1[:],
                                                  s0=8.0, s1=4.0, imm2=0.0),
                reads=[b_t1], writes=[b_t2], inc=SV, tag=tg + "qlb")
        pr.step("vector", lambda e: e._custom_dve(ops["ANT_QLAD_C"], out=t_t1[:], in0=t_t2[:],
                                                  in1=t_sp[:], s0=0.5, s1=2.0, imm2=0.0),
                reads=[b_t2, b_sp], writes=[b_t1], inc=SV, tag=tg + "qlc")
        def e_dq(e):
            for c in range(16):
                inst = e._custom_dve(ops["ANT_QDEQ"], out=t_rp[:, c * P:(c + 1) * P],
                                     in0=t_t1[:, c * P:(c + 1) * P],
                                     in1=t_x[:, c * P:(c + 1) * P],
                                     s0=t_s0col[:, c:c + 1], s1=0.0, imm2=0.0)
            return inst
        pr.step("vector", e_dq, reads=[b_t1, b_s0col] + bx_list, writes=[b_rp], inc=SV,
                tag=tg + "qdq")

    def s6(st):
        t_eyB2, b_eyB2 = sbuf("eyeB6", [P, P], BF16, st)
        pr.step("sync", lambda e: e.dma_start(t_eyB2[:], d_eyeb[:, :]), writes=[b_eyB2],
                inc=SD, tag="eyebld6")
        t_wf, b_wf = [None] * 2, [None] * 2
        t_wb, b_wb = [None] * 2, [None] * 2
        t_wam, b_wam = [None] * 2, [None] * 2
        t_wrc, b_wrc = [None] * 2, [None] * 2
        t_ws0, b_ws0 = [None] * 2, [None] * 2
        t_wev, b_wev = [None] * 2, [None] * 2
        for i in range(2):
            t_wf[i], b_wf[i] = sbuf(f"wqst{i}", [P, 2048], F32, st)
            t_wb[i], b_wb[i] = sbuf(f"wqbf{i}", [P, 2048], BF16, st)
            t_wam[i], b_wam[i] = sbuf(f"wam{i}", [P, 16], F32, st)
            t_wrc[i], b_wrc[i] = sbuf(f"wrc{i}", [P, 16], F32, st)
            t_ws0[i], b_ws0[i] = sbuf(f"ws0{i}", [P, 16], F32, st)
            t_wev[i], b_wev[i] = sbuf(f"wev{i}", [P, 512], BF16, st)
        ps_tb, pb_tb = [None] * 2, [None] * 2
        for i in range(2):
            ps_tb[i], pb_tb[i] = psum(f"ps6tb{i}", [P, 512], BF16, st)
        t_rp, b_rp = _q["rp"]
        for wt in range(16):
            par = wt % 2
            pr.step("sync", lambda e, wt=wt, par=par: e.dma_start(
                t_wf[par][:], dW["Wf"][wt * P:(wt + 1) * P, :]),
                writes=[b_wf[par]], inc=SD, tag=f"wql{wt}")
            pr.step("vector", lambda e, par=par: e.tensor_reduce(
                t_wam[par][:, :], t_wf[par][:].rearrange("p (c f) -> p c f", f=P), AX.X,
                ALU.max, apply_absolute_value=True),
                reads=[b_wf[par]], writes=[b_wam[par]], inc=SV, tag=f"wam{wt}", settle=True)
            pr.step("vector", lambda e, par=par: e.reciprocal(t_wrc[par][:, :],
                                                              t_wam[par][:, :]),
                    reads=[b_wam[par]], writes=[b_wrc[par]], inc=SV, tag=f"wrc{wt}",
                    settle=True)
            def e_rs(e, par=par):
                e.tensor_scalar(t_wrc[par][:, :], t_wrc[par][:, :], 448.0, None, ALU.mult)
                return e.tensor_scalar(t_ws0[par][:, :], t_wam[par][:, :],
                                       cN["r448"][:, 0:1], None, ALU.mult)
            pr.step("vector", e_rs, reads=[b_wrc[par], b_wam[par], cB["r448"]],
                    writes=[b_wrc[par], b_ws0[par]], inc=SV, tag=f"wrs{wt}", settle=True)
            qtile(t_wf[par], [b_wf[par]], t_ws0[par], b_ws0[par], t_wrc[par], b_wrc[par],
                  tg=f"w{wt}")
            pr.step("scalar", lambda e, par=par: e.activation(t_wb[par][:], t_rp[:],
                                                              AF.Identity),
                    reads=[b_rp], writes=[b_wb[par]], inc=SS, tag=f"wcst{wt}")
            for g in range(4):
                gp = g % 2
                def e_wtr(e, g=g, gp=gp, par=par):
                    return [e.transpose(ps_tb[gp][:, j * P:(j + 1) * P],
                                        t_wb[par][:, (4 * g + j) * P:(4 * g + j + 1) * P],
                                        t_eyB2[:]) for j in range(4)]
                pr.step("tensor", e_wtr, reads=[b_wb[par], b_eyB2], writes=[pb_tb[gp]],
                        inc=SP, ninc=4, tag=f"wtr{wt}_{g}")
                pr.step("scalar", lambda e, gp=gp: e.activation(t_wev[gp][:],
                                                                ps_tb[gp][:, :],
                                                                AF.Identity),
                        reads=[pb_tb[gp]], writes=[b_wev[gp]], inc=SS, tag=f"wte{wt}_{g}")
                def e_wto(e, g=g, gp=gp, wt=wt):
                    return [e.dma_start(
                        d_wscr[(4 * g + j) * P:(4 * g + j + 1) * P, wt * P:(wt + 1) * P],
                        t_wev[gp][:, j * P:(j + 1) * P]) for j in range(4)]
                pr.step("sync", e_wto, reads=[b_wev[gp]], inc=SD, ninc=4,
                        tag=f"wto{wt}_{g}")

    def s7(st, t_x1, b_x1, t_xdeqT, b_xdeqT):
        t_wbf, b_wbf = sbuf("f_wbf", [P, NDC * NQ], BF16, st)
        t_fbf, b_fbf = sbuf("fbf", [P, NDC * NQ], BF16, st)
        ps_p, pb_p = [None] * 2, [None] * 2
        for i in range(2):
            ps_p[i], pb_p[i] = psum(f"f_ps{i}", [P, 512], F32, st)
        for half in range(2):
            def e_wld(e, half=half):
                return [e.dma_start(
                    t_wbf[:, dc * NQ:(dc + 1) * NQ],
                    d_wscr[dc * P:(dc + 1) * P, half * NQ:(half + 1) * NQ])
                    for dc in range(NDC)]
            pr.step("sync", e_wld, writes=[b_wbf], inc=SD, ninc=NDC,
                    tag=f"fwld{half}")
            for jc in range(8):
                oc = half * 8 + jc
                for tt in range(NTT):
                    par = (oc * NTT + tt) % 2
                    def e_mm(e, jc=jc, tt=tt, par=par):
                        for dc in range(NDC):
                            inst = e.matmul(
                                ps_p[par][:, :],
                                t_wbf[:, dc * NQ + jc * P: dc * NQ + jc * P + P],
                                t_xdeqT[:, dc * NQ + tt * 512: dc * NQ + tt * 512 + 512],
                                start=(dc == 0), stop=(dc == NDC - 1))
                        return inst
                    pr.step("tensor", e_mm, reads=[b_wbf, b_xdeqT], writes=[pb_p[par]],
                            inc=SP, tag=f"fmm{oc}_{tt}")
                    pr.step("vector", lambda e, oc=oc, tt=tt, par=par: e.tensor_copy(
                        t_fbf[:, oc * NQ + tt * 512: oc * NQ + tt * 512 + 512],
                        ps_p[par][:, :]),
                        reads=[pb_p[par]], writes=[b_fbf], inc=SV, tag=f"fev{oc}_{tt}")
        ln_res(st, "lnf", t_fbf, b_fbf, None, (t_x1, b_x1), t_x1, b_x1)
        def e_st(e):
            return [e.dma_start(d_out[dc * P:(dc + 1) * P, :],
                                t_x1[:, dc * NQ:(dc + 1) * NQ]) for dc in range(NDC)]
        pr.step("sync", e_st, reads=[b_x1], inc=SD, ninc=NDC, tag="outst")

    from contextlib import ExitStack as _ES
    ALLC = [SV, SS, SP, SD, SG, SL]
    s0()
    want = {"s1": 1, "s2": 2, "s3": 3, "s4": 4, "s5": 5, "s6": 6, "all": 9}[upto]

    st_qkv = _ES()
    st_ln1k = _ES()
    t_ln1k, b_ln1k = sbuf("ln1k", [P, NDC * NQ], BF16, st_ln1k, side="left")
    st_ln1q = _ES()
    t_ln1q, b_ln1q = sbuf("ln1q", [P, NDC * NQ], BF16, st_ln1q, side="left")
    _open = [st_ln1q, st_ln1k]

    def _close(stk):
        if stk in _open:
            _open.remove(stk)
            stk.close()
    with _ES() as st1:
        s1_set("l1q", d_xT, t_ln1q, b_ln1q, st1)
    pr.barrier(ALLC)
    with _ES() as st1b:
        s1_set("l1k", d_xkT, t_ln1k, b_ln1k, st1b)
    pr.barrier(ALLC)
    tap("ln1q", t_ln1q, b_ln1q)
    tap("ln1k", t_ln1k, b_ln1k)
    if want >= 2:
        t_QT, b_QT = sbuf("QT", [P, H * NQ], BF16, st_qkv)
        with _ES() as st2:
            s2_qk(st2, "WqT", "q", d_cosq, d_sinq, t_ln1q, b_ln1q, t_QT, b_QT)
        pr.barrier(ALLC)
        _close(st_ln1q)
        t_KT, b_KT = sbuf("KT", [P, H * NQ], BF16, st_qkv)
        with _ES() as st2b:
            s2_qk(st2b, "WkT", "k", d_cosk, d_sink, t_ln1k, b_ln1k, t_KT, b_KT)
        pr.barrier(ALLC)
        t_V, b_V = sbuf("Vtok", [P, NKC * 2048], BF16, st_qkv)
        with _ES() as st2c:
            s2_v(st2c, t_ln1k, b_ln1k, t_V, b_V)
        pr.barrier(ALLC)
        _close(st_ln1k)
        tap("QT", t_QT, b_QT)
        tap("KT", t_KT, b_KT)
        tap("V", t_V, b_V)
    st_at = _ES()
    _open.insert(0, st_at)
    if want >= 3:
        _close(st_ln1q)
        _close(st_ln1k)
        t_AT, b_AT = sbuf("AT", [P, H * NQ], BF16, st_at, side="left")
        with _ES() as st3:
            s3(st3, t_QT, b_QT, t_KT, b_KT, t_V, b_V, t_AT, b_AT)
        pr.barrier(ALLC)
        st_qkv.close()
        tap("AT", t_AT, b_AT)
    else:
        st_qkv.close()
    if want >= 4:
        st_abf = _ES()
        t_abf, b_abf = sbuf("abf", [P, NDC * NQ], BF16, st_abf)
        with _ES() as st4:
            s4_proj(st4, t_AT, b_AT, t_abf, b_abf)
        pr.barrier(ALLC)
        _close(st_at)
        t_x1, b_x1 = sbuf("x1T", [P, NDC * NQ], F32, side="left")
        with _ES() as st4b:
            ln_res(st4b, "lna", t_abf, b_abf, d_xT, None, t_x1, b_x1)
        pr.barrier(ALLC)
        st_abf.close()
        tap("x1", t_x1, b_x1)
    else:
        _close(st_at)
    if want >= 5:
        t_xdeqT, b_xdeqT = sbuf("xdeqT", [P, NDC * NQ], BF16, side="left")
        st56 = _ES()
        _q_stack[0] = st56
        for nm in ["v", "l", "sp", "rp", "t1", "t2"]:
            t, b = sbuf("q_" + nm, [P, 2048], F32, st56)
            _q[nm] = (t, b, Buf("q_v_hi")) if nm == "v" else (t, b)
        with _ES() as st5:
            s5(st5, t_x1, b_x1, t_xdeqT, b_xdeqT)
        pr.barrier(ALLC)
        tap("xdeqT", t_xdeqT, b_xdeqT)
    if want >= 6:
        with _ES() as st6:
            s6(st6)
        pr.barrier(ALLC)
    if want >= 9:
        st56.close()
        pr.barrier(ALLC)
        with _ES() as st7:
            s7(st7, t_x1, b_x1, t_xdeqT, b_xdeqT)
        tap("out", t_x1, b_x1)
    elif want >= 5:
        st56.close()
    for stk in list(_open):
        _close(stk)

    fin_sd = SD.n
    fin = {"vector": SV.n, "scalar": SS.n, "tensor": SP.n, "gpsimd": SG.n}
    with es:
        blk = es.enter_context(nc.Block())

        @blk.sync
        def _(sync):
            pr.emit("sync", sync)
            sync.wait_ge(SD.sem, fin_sd)
            for cn, (ctr, nm) in [("v", (SV, "vector")), ("s", (SS, "scalar")),
                                  ("t", (SP, "tensor")), ("g", (SG, "gpsimd")),
                                  ("l", (SL, "gpsimd"))]:
                if ctr.n:
                    sync.wait_ge(ctr.sem, ctr.n)

        @blk.vector
        def _(vector):
            pr.emit("vector", vector)

        @blk.scalar
        def _(scalar):
            pr.emit("scalar", scalar)

        @blk.tensor
        def _(tensor):
            pr.emit("tensor", tensor)

        @blk.gpsimd
        def _(gpsimd):
            pr.emit("gpsimd", gpsimd)

    mybir.codegen_inst_isa_subclasses(nc)
    return nc, dbg_out


HH, DD, DM, TSEQ = 16, 128, 2048, 2048
SB, RB = 204, 819
EVICT = TSEQ - RB
KEPT_IDX = np.concatenate([np.arange(SB), np.arange(EVICT, TSEQ)])
NQ = 1024
BF = ml_dtypes.bfloat16


def make_mask():
    m = np.zeros((128, 16), np.float32)
    for kc in range(8):
        ones = np.ones(128, np.float32)
        if kc == 7:
            ones[127] = 0.0
        ind = np.zeros(128, np.float32)
        if kc == 1:
            ind[77:] = 1.0
        elif kc >= 2:
            ind[:] = 1.0
        if kc == 7:
            ind[127] = 0.0
        m[:, 2 * kc] = ones
        m[:, 2 * kc + 1] = ind
    return m.astype(BF)


def prep_core(core, x, Wq, Wk, Wv, Wo, Wf, cos32, sin32, u_ref):
    b, half = core // 2, core % 2
    q0 = NQ * half
    sgn = np.ones((DD, 1), np.float32)
    sgn[:64] = -1.0
    qpos = np.arange(q0, q0 + NQ)
    kpos = np.concatenate([KEPT_IDX, [0]])
    xk = x[b][KEPT_IDX]
    xk_pad = np.concatenate([xk, np.zeros((1, DM), np.float32)], 0)
    m = {
        "xT": np.ascontiguousarray(x[b, q0:q0 + NQ].T),
        "xkT": np.ascontiguousarray(xk_pad.T),
        "WqT": np.ascontiguousarray(Wq.T),
        "WkT": np.ascontiguousarray(Wk.T),
        "WvT": np.ascontiguousarray(Wv.T),
        "WoT": np.ascontiguousarray(Wo.T),
        "Wf": np.ascontiguousarray(Wf),
        "cosq": np.ascontiguousarray(cos32[qpos].T),
        "sinq": np.ascontiguousarray(sin32[qpos].T * sgn),
        "cosk": np.ascontiguousarray(cos32[kpos].T),
        "sink": np.ascontiguousarray(sin32[kpos].T * sgn),
        "mask": make_mask(),
        "urow": (u_ref[b:b + 1].astype(np.float32) if half == 1
                 else np.full((1, HH), 1e30, np.float32)),
        "eyef": np.eye(128, dtype=np.float32),
        "eyeb": np.eye(128, dtype=np.float32).astype(BF),
    }
    return m


def featmat(dev):
    return np.concatenate([dev[:, dc * NQ:(dc + 1) * NQ] for dc in range(16)], axis=0)


_CACHE = {}


def kernel(L0_attn_hidden_states, L0_attn_Wq, L0_attn_Wk, L0_attn_Wv, L0_attn_Wo,
           L0_ffn_weight):
    x = np.ascontiguousarray(L0_attn_hidden_states, dtype=np.float32)
    cos32, sin32, u_ref = _make_tables()
    if "nc" not in _CACHE:
        _CACHE["nc"] = build()[0]
    nc = _CACHE["nc"]
    in_maps = [prep_core(c, x, L0_attn_Wq, L0_attn_Wk, L0_attn_Wv, L0_attn_Wo,
                         L0_ffn_weight, cos32, sin32, u_ref) for c in range(8)]
    res = run_bass_kernel_spmd(nc, in_maps, core_ids=list(range(8)))
    out = np.empty((4, 2048, 2048), np.float32)
    for c in range(8):
        b, half = c // 2, c % 2
        out[b, 1024 * half:1024 * half + 1024] = res.results[c]["outT"].T
    return out
